# version 09
# baseline (speedup 1.0000x reference)
import numpy as np
import ml_dtypes
import concourse.bass as bass
import concourse.mybir as mybir
from concourse.bass_utils import run_bass_kernel_spmd

F32 = mybir.dt.float32
BF16 = mybir.dt.bfloat16
AF = mybir.ActivationFunctionType
ALU = mybir.AluOpType
AX = mybir.AxisListType
BF = ml_dtypes.bfloat16

DBG = {}

B, S, D = 2, 8192, 2048
TQ = 2048
HALO = 512
XT = HALO + TQ
NEG = -1e30

_NSLOT = 40
_SEM_STEP = {"pe": 1, "dve": 1, "act": 1}


class _Op:
    __slots__ = ("unit", "sem", "fn", "waits", "cnt", "pos", "tag", "slot",
                 "drain")


class _Slot:
    __slots__ = ("name", "uses", "waiters", "last")

    def __init__(self, name):
        self.name = name
        self.uses = 0
        self.waiters = {}
        self.last = None


class Sched:
    def __init__(self):
        self.ops = []
        self._cnt = {k: 0 for k in _SEM_STEP}
        self.join = []
        self.slots = {"sp": [_Slot(f"sp{i}") for i in range(_NSLOT)],
                      "gp": [_Slot(f"gp{i}") for i in range(_NSLOT)]}
        self._rr = {"sp": 0, "gp": 0}
        self.ded = {}
        self.sem_names = set(_SEM_STEP) | {s.name for p in self.slots.values()
                                           for s in p}

    def _collect(self, waits, unit, flush=()):
        ws = []
        for w in waits:
            if w is None:
                continue
            if isinstance(w, (list, tuple)):
                ws.extend(x for x in w if x is not None)
            else:
                ws.append(w)
        ws.extend(self.join)
        best = {}
        for w in ws:
            if w.unit == unit:
                continue
            if w.sem not in best or w.cnt > best[w.sem].cnt:
                best[w.sem] = w
        for w in flush:
            if w is None:
                continue
            if w.sem not in best or w.cnt > best[w.sem].cnt:
                best[w.sem] = w
        return best

    def add(self, sem, fn, waits=(), tag="", flush=(), drain=False):
        assert sem in _SEM_STEP
        unit = sem
        best = self._collect(waits, unit, flush)
        op = _Op()
        op.unit, op.sem, op.fn, op.tag, op.slot = unit, sem, fn, tag, None
        op.drain = drain
        op.waits = list(best.values())
        op.pos = len(self.ops)
        if fn is not None:
            self._cnt[sem] += _SEM_STEP[sem]
        op.cnt = self._cnt[sem]
        for w in op.waits:
            assert w.pos < op.pos, f"wait on future op {w.tag} from {tag}"
            if w.slot is not None:
                sl = w.slot
                prev = sl.waiters.get(op.sem)
                if prev is None or op.cnt >= prev.cnt:
                    sl.waiters[op.sem] = op
        self.ops.append(op)
        return op

    def _add_dma_one(self, q, fn, best, tag, ded):
        op = _Op()
        op.unit, op.fn, op.tag = q, fn, tag
        op.pos = len(self.ops)
        if ded is not None:
            nm = "d_" + ded
            if nm not in self.ded:
                self.ded[nm] = [nm, 0]
                self.sem_names.add(nm)
            self.ded[nm][1] += 16
            op.sem = nm
            op.cnt = self.ded[nm][1]
            op.slot = None
        else:
            pool = self.slots[q]
            sl = pool[self._rr[q] % _NSLOT]
            self._rr[q] += 1
            for wop in sl.waiters.values():
                if wop.sem not in best or wop.cnt > best[wop.sem].cnt:
                    best[wop.sem] = wop
            sl.waiters = {}
            sl.uses += 1
            op.sem = sl.name
            op.cnt = 16 * sl.uses
            op.slot = sl
            sl.last = op
        op.waits = [w for w in best.values() if w is not op]
        for w in op.waits:
            assert w.pos < op.pos, f"wait on future op {w.tag} from {tag}"
            if w.slot is not None and w.slot is not op.slot:
                prev = w.slot.waiters.get("__q_" + op.sem)
                if prev is None or op.cnt >= prev.cnt:
                    w.slot.waiters["__q_" + op.sem] = op
        self.ops.append(op)
        return op

    def add_dma(self, q, fn, waits=(), tag="", ded=None, dup=False):
        assert q in ("sp", "gp") or (q == "ac" and ded is not None)
        best = self._collect(waits, q)
        op = self._add_dma_one(q, fn, best, tag, ded)
        if dup:
            if op.slot is not None:
                sl = op.slot
                self._rr[q] -= 1 if False else 0
                op2 = _Op()
                op2.unit, op2.fn, op2.tag = q, fn, tag + "_d"
                op2.pos = len(self.ops)
                sl.uses += 1
                op2.sem = sl.name
                op2.cnt = 16 * sl.uses
                op2.slot = sl
                op2.waits = []
                sl.last = op2
                self.ops.append(op2)
                op = op2
            else:
                op = self._add_dma_one(q, fn, {}, tag + "_d", ded)
        return op

    def set_join(self, since_pos):
        last = {}
        for opx in self.ops[since_pos:]:
            if opx.fn is not None:
                last[opx.sem] = opx
        for w in self.join:
            if w.sem not in last or w.cnt > last[w.sem].cnt:
                last[w.sem] = w
        self.join = list(last.values())

    def final_ops(self):
        last = {}
        for o_ in self.ops:
            if o_.fn is not None:
                last[o_.sem] = o_
        return list(last.values())

    def emit(self, units, eng, sems):
        if isinstance(units, str):
            units = (units,)
        waited = {}
        for op in self.ops:
            if op.unit not in units:
                continue
            for w in op.waits:
                if w.cnt > waited.get(w.sem, 0):
                    eng.wait_ge(sems[w.sem], w.cnt)
                    waited[w.sem] = w.cnt
            if op.fn is None:
                continue
            inst = op.fn(eng)
            assert inst is not None, f"op {op.tag} emitted nothing"
            if op.unit in ("sp", "gp", "ac"):
                inst.then_inc(sems[op.sem], 16)
            elif op.unit == "pe":
                inst.then_inc(sems[op.sem], 1)
            else:
                eng.drain()
                eng.sem_inc(sems[op.sem], 1)


class Buf:
    def __init__(self):
        self.w = None
        self.r = {}

    def rd(self):
        return [self.w] if self.w is not None else []

    def wr(self):
        out = list(self.r.values())
        if self.w is not None:
            out.append(self.w)
        return out

    def did_read(self, op):
        prev = self.r.get(op.sem)
        if prev is None or op.cnt > prev.cnt:
            self.r[op.sem] = op

    def did_write(self, op):
        self.w = op
        self.r = {}


def _stage_weights(inp):
    st = {}

    def c(a):
        return np.ascontiguousarray(a, dtype=np.float32)

    st["wq"] = c(inp["L0_attn_Wq"].reshape(16, 128, 16, 128).transpose(0, 3, 2, 1))
    st["wk"] = c(inp["L0_attn_Wk"].reshape(16, 128, 16, 128).transpose(0, 3, 2, 1))
    st["wv"] = c(inp["L0_attn_Wv"].reshape(8, 512, 16, 128).transpose(0, 3, 2, 1))
    st["wo"] = c(inp["L0_attn_Wo"].reshape(4, 512, 32, 128).transpose(0, 3, 2, 1))
    st["wcq"] = c(inp["L0_attn_Wq_conv"].reshape(16, 128, 128, 4).transpose(0, 2, 3, 1))
    st["wck"] = c(inp["L0_attn_Wk_conv"].reshape(16, 128, 128, 4).transpose(0, 2, 3, 1))
    st["bcq"] = c(inp["L0_attn_bq_conv"].reshape(16, 128).T)
    st["bck"] = c(inp["L0_attn_bk_conv"].reshape(16, 128).T)
    st["wg"] = c(inp["L0_attn_Wg"].T.reshape(16, 128, 16).transpose(1, 0, 2))
    st["wb"] = c(inp["L0_attn_Wb"].T.reshape(16, 128, 16).transpose(1, 0, 2))
    st["bgc"] = c(inp["L0_attn_bg"].reshape(16, 1))
    st["bbc"] = c(inp["L0_attn_bb"].reshape(16, 1))
    st["wgate"] = c(inp["L0_ffn_gate_w"].T.reshape(16, 128, 8).transpose(1, 0, 2))
    st["gbias"] = c(inp["L0_ffn_gate_bias"].reshape(1, 8))
    st["sgw"] = c(inp["L0_ffn_shared_gw"].reshape(44, 128, 16, 128).transpose(0, 3, 2, 1))
    st["sdw"] = c(inp["L0_ffn_shared_dw"].reshape(4, 512, 22, 128).transpose(0, 3, 2, 1))
    st["wup"] = c(inp["L0_ffn_up_w"].reshape(8, 22, 128, 16, 128).transpose(0, 1, 4, 3, 2))
    st["wdn"] = c(inp["L0_ffn_down_w"].reshape(8, 4, 512, 11, 128)
                  .transpose(0, 1, 4, 3, 2))
    st["cmask"] = np.where(np.arange(128)[:, None] <= np.arange(128)[None, :],
                           np.float32(0.0), np.float32(NEG)).astype(np.float32)
    st["ident"] = np.eye(128, dtype=BF)
    st["identf"] = np.eye(128, dtype=np.float32)
    st["ones"] = np.ones((1, 128), dtype=BF)
    sel = np.zeros((16, 16 * 128), dtype=BF)
    for h in range(16):
        sel[h, h * 128:(h + 1) * 128] = 1.0
    st["sel"] = sel
    return st


_WSHAPES = [
    ("wq", [16, 128, 16, 128]), ("wk", [16, 128, 16, 128]),
    ("wv", [8, 128, 16, 512]), ("wo", [4, 128, 32, 512]),
    ("wcq", [16, 128, 4, 128]), ("wck", [16, 128, 4, 128]),
    ("bcq", [128, 16]), ("bck", [128, 16]),
    ("wg", [128, 16, 16]), ("wb", [128, 16, 16]),
    ("bgc", [16, 1]), ("bbc", [16, 1]),
    ("wgate", [128, 16, 8]), ("gbias", [1, 8]),
    ("sgw", [44, 128, 16, 128]), ("sdw", [4, 128, 22, 512]),
    ("wup", [8, 22, 128, 16, 128]), ("wdn", [8, 4, 128, 11, 512]),
    ("cmask", [128, 128]),
]


def build_program():
    nc = bass.Bass("TRN2", target_bir_lowering=False, debug=False)
    sch = Sched()

    xc = nc.dram_tensor("xc", [XT, D], F32, kind="ExternalInput").ap()
    hmask_d = nc.dram_tensor("hmask", [128, 1], F32, kind="ExternalInput").ap()
    din = {}
    for nm, shp in _WSHAPES:
        din[nm] = nc.dram_tensor(nm, shp, F32, kind="ExternalInput").ap()
    ident_d = nc.dram_tensor("ident", [128, 128], BF16, kind="ExternalInput").ap()
    identf_d = nc.dram_tensor("identf", [128, 128], F32, kind="ExternalInput").ap()
    ones_d = nc.dram_tensor("ones", [1, 128], BF16, kind="ExternalInput").ap()
    sel_d = nc.dram_tensor("sel", [16, 16 * 128], BF16, kind="ExternalInput").ap()
    out_d = nc.dram_tensor("out", [TQ, D], F32, kind="ExternalOutput").ap()
    dbg_outs = {}
    _dbg_shapes = {"xnb": ([XT, D], BF16), "vb": ([XT, 4096], BF16),
                   "partial": ([TQ, D], F32), "SH": ([128, 16 * 256], F32),
                   "qg0": ([128, 16 * 512], BF16), "kg0": ([128, 16 * 512], BF16),
                   "ogrp0": ([128, 32 * 512], BF16), "acc0": ([128, 8 * 2048], F32),
                   "wes0": ([128, 64], F32), "xng0": ([128, 16 * 512], BF16),
                   "preq0": ([128, 16480], BF16), "Sfin": ([128, 16 * 256], F32),
                   "gbb": ([128, 8], F32), "rta": ([128, 8], F32),
                   "lbH": ([16, 512], F32), "lnbH": ([16, 512], F32),
                   "decH": ([16, 256], F32), "dectmH": ([128, 32], F32),
                   "acbH": ([128, 32], F32), "KhH": ([128, 256], BF16),
                   "vbufH": ([128, 8192], BF16), "zbH": ([16, 512], F32),
                   "accsh": ([128, 8 * 2048], F32),
                   "rtb": ([128, 8], F32), "rtc": ([128, 8], F32),
                   "rtd": ([128, 8], F32), "xstart": ([128, 2048], F32),
                   "xend": ([128, 2048], F32), "snapx": ([128, 2048], F32),
                   "stat0": ([128, 16], F32)}
    for nm_ in DBG:
        shp_, dt_ = _dbg_shapes[nm_]
        dbg_outs[nm_] = nc.dram_tensor("dbg_" + nm_, shp_, dt_,
                                       kind="ExternalOutput").ap()

    xnb = nc.dram_tensor("xnb", [XT, D], BF16, kind="Internal").ap()
    vb = nc.dram_tensor("vbd", [XT, 4096], BF16, kind="Internal").ap()
    partial = nc.dram_tensor("partiald", [TQ, D], F32, kind="Internal").ap()
    wbf = {}
    for nm in ["wq", "wk", "wv", "wo", "wcq", "wck", "sgw", "sdw", "wup", "wdn"]:
        wbf[nm] = nc.dram_tensor(nm + "_b", list(din[nm].shape), BF16,
                                 kind="Internal").ap()

    guards = []

    def sb(name, shape, dtype):
        g = nc.sbuf_tensor(name, shape, dtype)
        t = g.__enter__()
        guards.append(g)
        return t

    def psa(name, shape, dtype=F32):
        g = nc.psum_tensor(name, shape, dtype)
        t = g.__enter__()
        guards.append(g)
        return t

    ident = sb("identsb", [128, 128], BF16)
    identf = sb("identfsb", [128, 128], F32)
    onesb = sb("onessb", [1, 128], BF16)
    selb = sb("selsb", [16, 16 * 128], BF16)
    cmask = sb("cmasksb", [128, 128], F32)
    hmk = sb("hmksb", [128, 1], F32)
    bcq_s = sb("bcqs", [128, 16], F32)
    bck_s = sb("bcks", [128, 16], F32)
    bgc_s = sb("bgcs", [16, 1], F32)
    bbc_s = sb("bbcs", [16, 1], F32)
    wg_s = sb("wgs", [128, 16 * 16], BF16)
    wb_s = sb("wbs", [128, 16 * 16], BF16)
    wgt_s = sb("wgts", [128, 16 * 8], BF16)
    gb_row = sb("gbrow", [1, 8], F32)
    gb_hi = sb("gbhi", [1, 8], BF16)
    gb_lo = sb("gblo", [1, 8], BF16)
    gb_b = sb("gbb", [128, 8], F32)
    gb_pad = sb("gbpad", [16, 8], F32)
    gb_padb = sb("gbpadb", [16, 8], BF16)
    ones16 = sb("ones16", [16, 128], BF16)
    cstc = sb("cstc", [128, 4], F32)
    sv_q = sb("svq", [128, 48], BF16)
    sv_k = sb("svk", [128, 48], BF16)
    ringB = sb("ringBs", [128, 20 * 512], BF16)
    RB = 20

    psb = [psa(f"psb{i}", [128, 512], F32) for i in range(7)]
    psT = psa("psTb", [128, 512], BF16)

    bufs = {}

    def B_(key):
        if key not in bufs:
            bufs[key] = Buf()
        return bufs[key]

    ringAB = [B_(("rA", i)) for i in range(64)]
    ringBB = [B_(("rB", i)) for i in range(RB)]
    rpos = {"A": 0, "B": 0}

    def tap(nm_, src_ap, waits):
        if nm_ not in DBG:
            return None
        return dma("sp", dbg_outs[nm_], src_ap, waits, "tap_" + nm_)

    def dma(q, out, in_, waits=(), tag="", dup=False, ded=None, **kw):
        def fn(e, out=out, in_=in_, kw=kw):
            return e.dma_start(out=out, in_=in_, **kw)
        return sch.add_dma(q, fn, waits, tag, ded=ded, dup=dup)

    op = sch.add

    if "xstart" in DBG:
        dma("sp", dbg_outs["xstart"], xc[0:128, :], (), "tap_xstart")
    init_ops = []
    for dst, src in [(ident, ident_d), (identf, identf_d), (onesb, ones_d),
                     (selb, sel_d), (cmask, din["cmask"]), (hmk, hmask_d),
                     (bcq_s, din["bcq"]), (bck_s, din["bck"]),
                     (bgc_s, din["bgc"]), (bbc_s, din["bbc"]),
                     (gb_row, din["gbias"])]:
        o = dma("sp", dst[tuple(slice(None) for _ in dst.shape)], src, (), "init",
                ded="init")
        init_ops.append(o)
    for dst, src in [(wg_s, din["wg"].rearrange("p a b -> p (a b)")),
                     (wb_s, din["wb"].rearrange("p a b -> p (a b)")),
                     (wgt_s, din["wgate"].rearrange("p a b -> p (a b)"))]:
        o = dma("ac", dst[:, :], src, (), "initw", ded="initw")
        init_ops.append(o)

    c0_ = op("dve", lambda e: e.memset(cstc[:, 0:1], 1e-5), (), "cst0")
    c1_ = op("dve", lambda e: e.memset(cstc[:, 1:2], -1.0), [c0_], "cst1")
    w0_ = op("act", lambda e: e.activation(cstc[:, 2:3], cstc[:, 0:1], AF.Square),
             [c1_], "warm_sq")
    w1_ = op("act", lambda e: e.activation(cstc[:, 2:3], cstc[:, 0:1], AF.Exp),
             [w0_], "warm_exp")
    w2_ = op("act", lambda e: e.activation(cstc[:, 2:3], cstc[:, 0:1], AF.Silu),
             [w1_], "warm_silu")
    w3_ = op("act", lambda e: e.activation(cstc[:, 3:4], cstc[:, 0:1], AF.Square,
                                           accum_out=cstc[:, 2:3]), [w2_], "warm_acc")
    init_ops = init_ops + [c1_, w3_]
    gb16 = gb_hi
    o16 = op("dve", lambda e: e.memset(ones16[:, :], 1.0), (), "ones16i")
    o3a = op("dve", lambda e: e.memset(gb_pad[:, :], 0.0), init_ops, "gbz")
    o3b = dma("sp", gb_pad[0:1, 0:8], din["gbias"], init_ops + [o3a], "gbld",
              ded="init")
    o3c = op("dve", lambda e: e.tensor_copy(gb_padb[:, :], gb_pad[:, :]),
             [o3a, o3b], "gbcast")
    m1_ = op("pe", lambda e: e.matmul(psb[0][:, 0:8], ones16[:, :],
                                      gb_padb[:, :], start=True, stop=True),
             [o3c, o16] + init_ops + B_(("bank", 0)).wr(), "gbmm")
    B_(("bank", 0)).did_write(m1_)
    o3 = op("dve", lambda e: e.tensor_copy(gb_b[:, :], psb[0][:, 0:8]),
            [m1_], "gbev")
    B_(("bank", 0)).did_read(o3)
    init_ops = init_ops + [o3]
    tap("gbb", gb_b[:, :], [o3])

    prolog = {}
    _castq = []

    def _chunks(nm):
        csrc, cdst = din[nm], wbf[nm]
        n0 = csrc.shape[0]
        pat = {3: "a b c -> a (b c)",
               4: "a b c d -> a (b c d)",
               5: "a b c d e -> a (b c d e)"}[len(csrc.shape)]
        fs = csrc.rearrange(pat)
        fd = cdst.rearrange(pat)
        X = fs.shape[1]
        step = max(1, min(X, (1 << 18)))
        out = []
        for r in range(n0):
            i = 0
            while i < X:
                j = min(X, i + step)
                out.append((nm, fd[r:r + 1, i:j], fs[r:r + 1, i:j], False))
                i = j
        nm_, a_, b_, _ = out[-1]
        out[-1] = (nm_, a_, b_, True)
        return out

    def cast_eager(names):
        for nm in names:
            for nm_, fd_, fs_, last_ in _chunks(nm):
                o = dma("ac", fd_, fs_, (), f"cast_{nm_}", ded="cast")
                if last_:
                    prolog[nm_] = o

    def cast_lazy(names):
        for nm in names:
            _castq.extend(_chunks(nm))

    def pump_cast(k):
        while k > 0 and _castq:
            nm_, fd_, fs_, last_ = _castq.pop(0)
            o = dma("ac", fd_, fs_, (), f"cast_{nm_}", ded="cast")
            if last_:
                prolog[nm_] = o
            k -= 1

    def P(nm):
        while nm not in prolog:
            pump_cast(16)
        return prolog[nm]

    cast_eager(["wk", "wck", "wq", "wcq", "wv"])
    cast_lazy(["wo", "sgw", "sdw", "wup", "wdn"])

    x_t = sb("xt", [128, 2048], F32)
    xn_t = sb("xnt", [128, 2 * 2048], BF16)
    sqs = sb("sqs", [128, 256], F32)
    stat = sb("stat", [128, 16], F32)
    snapx_b = sb("snapxb", [128, 2048], F32) if "snapx" in DBG else None
    stat0_b = sb("stat0b", [128, 16], F32) if "stat0" in DBG else None

    ln_done = []
    prev_tail = []
    for t in range(20):
        par = t % 2
        xv = x_t[:, :]
        xnv = xn_t[:, par * 2048:(par + 1) * 2048]
        bx, bxn = B_("xt"), B_(("xnt", par))
        lo = dma("sp", xv, xc[t * 128:(t + 1) * 128, :], bx.wr() + prev_tail,
                 f"ldx{t}")
        bx.did_write(lo)
        if t == 0 and snapx_b is not None:
            sxp = op("dve", lambda e: e.tensor_copy(snapx_b[:, :], x_t[:, :]),
                     [lo], "snapx0")
            bx.did_read(sxp)
            tap("snapx", snapx_b[:, :], [sxp])
        s0 = op("dve", lambda e: e.reduce_sum(stat[:, 0:1], x_t[:, :], axis=AX.X),
                [lo] + prev_tail, f"lsum{t}")
        sq_ops = []
        for j in range(8):
            so = op("act", lambda e, j=j: e.activation(
                sqs[:, :], x_t[:, j * 256:(j + 1) * 256], AF.Square,
                accum_out=stat[:, 8 + j:9 + j]),
                [lo, s0] + (prev_tail if j == 0 else []), f"lsq{t}_{j}")
            sq_ops.append(so)
        wa1 = op("act", lambda e: e.copy(cstc[:, 3:4], cstc[:, 0:1]),
                 [sq_ops[-1]], f"lwa1{t}")
        wa2 = op("act", lambda e: e.copy(cstc[:, 3:4], cstc[:, 1:2]),
                 [wa1], f"lwa2{t}")
        s1 = op("dve", lambda e: e.reduce_sum(stat[:, 1:2], stat[:, 8:16], axis=AX.X),
                [wa2], f"lsq2{t}")
        s2 = op("dve", lambda e: e.tensor_scalar(
            stat[:, 2:3], stat[:, 0:1], 1.0 / D, None, ALU.mult), [s0], f"lmean{t}")
        s3 = op("dve", lambda e: e.tensor_scalar(
            stat[:, 3:4], stat[:, 1:2], 1.0 / D, None, ALU.mult), [s1], f"lm2{t}")
        s4 = op("dve", lambda e: e.scalar_tensor_tensor(
            stat[:, 4:5], stat[:, 2:3], -1.0, stat[:, 2:3], ALU.mult, ALU.mult),
            [s2], f"lmm{t}")
        s5 = op("dve", lambda e: e.tensor_add(stat[:, 5:6], stat[:, 3:4], stat[:, 4:5]),
                [s3, s4], f"lvar{t}")
        s6 = op("act", lambda e: e.activation(stat[:, 6:7], stat[:, 5:6], AF.Sqrt,
                                              bias=cstc[:, 0:1]), [s5] + [c1_], f"lsd{t}")
        s7 = op("dve", lambda e: e.reciprocal(stat[:, 7:8], stat[:, 6:7]), [s6], f"lrs{t}")
        if t == 0 and stat0_b is not None:
            stp = op("dve", lambda e: e.tensor_copy(stat0_b[:, :], stat[:, :]),
                     [s7], "stat0c")
            tap("stat0", stat0_b[:, :], [stp])
        p1 = op("dve", lambda e: e.tensor_copy(cstc[:, 3:4], cstc[:, 0:1]),
                [], f"lpad1{t}")
        p2 = op("dve", lambda e: e.tensor_copy(cstc[:, 3:4], cstc[:, 1:2]),
                [p1], f"lpad2{t}")
        s8 = op("dve", lambda e, xnv=xnv: e.tensor_scalar(
            xnv, x_t[:, :], stat[:, 2:3], stat[:, 7:8], ALU.subtract, ALU.mult),
            bxn.wr(), f"lap{t}", flush=[s7], drain=True)
        bxn.did_write(s8)
        bx.did_read(s8)
        st_ = dma("sp", xnb[t * 128:(t + 1) * 128, :], xnv, [s8], f"sxn{t}")
        bxn.did_read(st_)
        ln_done.append(st_)
        prev_tail = [s6, s7, s8, sq_ops[-1], s1]

    def ln_wait(t0, t1):
        return [ln_done[t] for t in range(t0 // 128, min(20, (t1 + 127) // 128))]

    p01_guards_start = len(guards)
    ringA = sb("ringAs", [128, 64 * 128], BF16)
    RA = 64
    arena = sb("arena", [128, 16480], BF16)
    pre_q = arena[:, 0:8240]
    pre_k = arena[:, 8240:16480]
    o_grp = arena[:, 0:16384]
    xn_g = sb("xng", [128, 16 * 512], BF16)
    qg = sb("qg", [128, 16 * 512], BF16)
    kg = sb("kg", [128, 16 * 512], BF16)
    vbuf = sb("vbuf", [128, 2 * 4096], BF16)
    vstage = sb("vstage", [128, 4 * 512], BF16)
    S_fp = sb("Sfp", [128, 16 * 256], F32)
    S_bf = sb("Sbf", [128, 256], BF16)
    zg_f = sb("zgf", [16, 512], F32)
    zb_f = sb("zbf", [16, 512], F32)
    lng_f = sb("lngf", [16, 512], F32)
    lnb_f = sb("lnbf", [16, 512], F32)
    lb_f = sb("lbf", [16, 512], F32)
    r1_f = sb("r1f", [16, 512], F32)
    r1h = sb("r1h", [16, 512], BF16)
    r1l = sb("r1l", [16, 512], BF16)
    lad = sb("lad", [16, 512], F32)
    dec_f = sb("decf", [16, 2 * 128], F32)
    dtmp = sb("dtmpf", [16, 128], F32)
    aC_c = sb("aCc", [16, 2], BF16)
    lb_tm = sb("lbtm", [128, 2 * 16], F32)
    dec_tm = sb("dectm", [128, 2 * 16], F32)
    acb = sb("acbs", [128, 2 * 16], F32)
    tmpD = sb("tmpD", [128, 2 * 128], F32)
    Dp_s = sb("Dps", [128, 2 * 128], F32)
    Eq_s = sb("Eqs", [128, 2 * 128], F32)
    A_bf = sb("Abf", [128, 2 * 128], BF16)
    Qb_bf = sb("Qbbf", [128, 2 * 128], BF16)
    Kh_bf = sb("Khbf", [128, 2 * 128], BF16)
    qg_f32 = qg.bitcast(F32)

    zpad = op("dve", lambda e: e.memset(lad[:, 0:128], 0.0), (), "zlad0")
    zpad2 = op("dve", lambda e: e.memset(lad[:, 256:384], 0.0), [zpad], "zlad1")
    zS = op("dve", lambda e: e.memset(S_fp[:, :], 0.0), [zpad2], "zS")
    B_("Sfull").did_write(zS)

    def statA_load(src_ap, nt, waits, tag):
        pump_cast(2)
        p = rpos["A"]
        if p + nt > RA:
            p = 0
        rpos["A"] = p + nt
        slots = ringAB[p:p + nt]
        w = list(waits)
        for s_ in slots:
            w += s_.wr()
        o = dma("gp", ringA[:, p * 128:(p + nt) * 128], src_ap, w, tag)
        for s_ in slots:
            s_.did_write(o)
        return p, slots, o

    def rhsB_load_multi(src2_ap, n, waits, tag, q="sp", batch=4):
        out = []
        i = 0
        while i < n:
            pump_cast(2)
            j = min(n, i + batch)
            if rpos["B"] % RB + (j - i) > RB:
                rpos["B"] += RB - rpos["B"] % RB
            p = rpos["B"] % RB
            rpos["B"] += j - i
            slots = ringBB[p:p + (j - i)]
            w = list(waits)
            for s_ in slots:
                w += s_.wr()
            o = dma(q, ringB[:, p * 512:(p + j - i) * 512],
                    src2_ap[:, i * 512:j * 512], w, f"{tag}_{i}")
            for k_, s_ in enumerate(slots):
                s_.did_write(o)
                out.append((p + k_, s_, o))
            i = j
        return out

    def rhsB_load(src_ap, waits, tag, q="sp"):
        p = rpos["B"] % RB
        rpos["B"] = p + 1
        slot = ringBB[p]
        o = dma(q, ringB[:, p * 512:(p + 1) * 512], src_ap,
                list(waits) + slot.wr(), tag)
        slot.did_write(o)
        return p, slot, o

    def emit_group(g, is_halo):
        gstart = 0 if is_halo else 512 + 512 * g
        tagg = "H" if is_halo else f"G{g}"
        p_start = len(sch.ops)

        def xnv(dc, lo, n):
            return xn_g[:, dc * 512 + lo: dc * 512 + lo + n]

        xn_ld = []
        bxg = B_("xng")
        for dc in range(16):
            o = dma("sp", xn_g[:, dc * 512:(dc + 1) * 512],
                    xnb[gstart:gstart + 512, dc * 128:(dc + 1) * 128],
                    ln_wait(gstart, gstart + 512) + (bxg.wr() if dc == 0 else []),
                    f"xnl{tagg}_{dc}", transpose=True)
            xn_ld.append(o)
        bxg.did_write(xn_ld[-1])

        def proj_pre(which, wname, pre_view, copy_pads):
            pb = B_((which, "pre"))
            sv = sv_q if which == "q" else sv_k
            bsv = B_(("sv", which))
            pads = []
            if copy_pads:
                o = op("dve", lambda e, pre_view=pre_view, sv=sv: e.tensor_copy(
                    pre_view.rearrange("p (oc c) -> p oc c", c=515)[:, :, 0:3],
                    sv.rearrange("p (oc c) -> p oc c", c=3)[:, :, 0:3]),
                    pb.rd() + pb.wr() + bsv.rd(), f"pad{which}{tagg}")
                bsv.did_read(o)
                pads.append(o)
            evs = []
            for oc in range(16):
                pslot, slots, wl = statA_load(
                    wbf[wname][oc].rearrange("p a b -> p (a b)"), 16,
                    [prolog[wname]], f"{wname}l{tagg}_{oc}")
                bank = psb[oc % 2]
                bps = B_(("bank", oc % 2))
                last = None
                for dc in range(16):
                    w = ([wl] + xn_ld + bps.wr() + pads) if dc == 0 else []
                    last = op("pe", lambda e, dc=dc, pslot=pslot, bank=bank: e.matmul(
                        bank[:, :], ringA[:, (pslot + dc) * 128:(pslot + dc + 1) * 128],
                        xnv(dc, 0, 512), start=(dc == 0), stop=(dc == 15)),
                        w, f"pj{which}{tagg}_{oc}_{dc}")
                for s_ in slots:
                    s_.did_read(last)
                bps.did_write(last)
                ev = op("act", lambda e, oc=oc, bank=bank, pre_view=pre_view: e.copy(
                    pre_view[:, oc * 515 + 3: oc * 515 + 515], bank[:, :]),
                    [last] + pads + (pb.wr() if oc == 0 else []),
                    f"pev{which}{tagg}_{oc}")
                bps.did_read(ev)
                evs.append(ev)
            pb.did_write(evs[-1])
            so_ = op("dve", lambda e, pre_view=pre_view, sv=sv: e.tensor_copy(
                sv.rearrange("p (oc c) -> p oc c", c=3)[:, :, 0:3],
                pre_view.rearrange("p (oc c) -> p oc c", c=515)[:, :, 512:515]),
                [evs[-1]] + bsv.wr(), f"sv{which}{tagg}")
            bsv.did_write(so_)
            pb.did_read(so_)
            evs.append(so_)
            return evs

        if is_halo:
            pbk = B_(("k", "pre"))
            mz = op("dve", lambda e: e.memset(
                pre_k.rearrange("p (oc c) -> p oc c", c=515)[:, :, 0:3], 0.0),
                pbk.wr() + [zS], "mspadk")
            pbk.did_write(mz)
            pre_done = proj_pre("k", "wk", pre_k, copy_pads=False)
            pbq = B_(("q", "pre"))
            for oc in range(16):
                pslot, slots, wl = statA_load(
                    wbf["wq"][oc].rearrange("p a b -> p (a b)"), 16,
                    [prolog["wq"]], f"wqpl_{oc}")
                bank = psb[2]
                bps = B_(("bank", 2))
                last = None
                for dc in range(16):
                    w = ([wl] + xn_ld + bps.wr()) if dc == 0 else []
                    last = op("pe", lambda e, dc=dc, pslot=pslot, bank=bank: e.matmul(
                        bank[:, 0:128],
                        ringA[:, (pslot + dc) * 128:(pslot + dc + 1) * 128],
                        xnv(dc, 384, 128), start=(dc == 0), stop=(dc == 15)),
                        w, f"qpro_{oc}_{dc}")
                for s_ in slots:
                    s_.did_read(last)
                bps.did_write(last)
                ev = op("act", lambda e, oc=oc, bank=bank: e.copy(
                    pre_q[:, oc * 515 + 387: oc * 515 + 515], bank[:, 0:128]),
                    [last] + (pbq.wr() if oc == 0 else []), f"qproev_{oc}")
                bps.did_read(ev)
                pre_done.append(ev)
            pbq.did_write(pre_done[-1])
            bsvq = B_(("sv", "q"))
            so_ = op("dve", lambda e: e.tensor_copy(
                sv_q.rearrange("p (oc c) -> p oc c", c=3)[:, :, 0:3],
                pre_q.rearrange("p (oc c) -> p oc c", c=515)[:, :, 512:515]),
                [pre_done[-1]] + bsvq.wr(), "svqH")
            bsvq.did_write(so_)
            pbq.did_read(so_)
            pre_done.append(so_)
        else:
            pre_done = proj_pre("q", "wq", pre_q, copy_pads=True)
            pre_done = pre_done + proj_pre("k", "wk", pre_k, copy_pads=True)

        def conv(which, wname, pre_view, bias_s, dst, bkey):
            evs = []
            for cg in range(16):
                pslot, slots, wl = statA_load(
                    wbf[wname][cg].rearrange("p a b -> p (a b)"), 4,
                    [prolog[wname]], f"{wname}l{tagg}_{cg}")
                bank = psb[2 + (cg % 2)]
                bps = B_(("bank", 2 + cg % 2))
                last = None
                for tau in range(4):
                    w = ([wl, pre_done[-1]] + bps.wr()) if tau == 0 else []
                    last = op("pe", lambda e, cg=cg, tau=tau, pslot=pslot, bank=bank,
                              pre_view=pre_view: e.matmul(
                        bank[:, :],
                        ringA[:, (pslot + tau) * 128:(pslot + tau + 1) * 128],
                        pre_view[:, cg * 515 + tau: cg * 515 + tau + 512],
                        start=(tau == 0), stop=(tau == 3)),
                        w, f"cv{which}{tagg}_{cg}_{tau}")
                for s_ in slots:
                    s_.did_read(last)
                bps.did_write(last)
                db = B_(bkey)
                ev = op("act", lambda e, cg=cg, bank=bank, dst=dst, bias_s=bias_s:
                        e.activation(dst[:, cg * 512: cg * 512 + 512], bank[:, :],
                                     AF.Silu, bias=bias_s[:, cg:cg + 1]),
                        [last] + (db.wr() if cg == 0 else []), f"cve{which}{tagg}_{cg}")
                bps.did_read(ev)
                evs.append(ev)
            B_(bkey).did_write(evs[-1])
            return evs

        if not is_halo:
            q_done = conv("q", "wcq", pre_q, bcq_s, qg, ("qg",))
        else:
            q_done = None
        k_done = conv("k", "wck", pre_k, bck_s, kg, ("kg",))
        if g == 0 and not is_halo:
            t_ = tap("qg0", qg[:, :], [q_done[-1]])
            if t_ is not None:
                B_(("qg",)).did_read(t_)
            t_ = tap("kg0", kg[:, :], [k_done[-1]])
            if t_ is not None:
                B_(("kg",)).did_read(t_)
            t_ = tap("xng0", xn_g[:, :], xn_ld)
            if t_ is not None:
                B_("xng").did_read(t_)
            t_ = tap("preq0", arena[:, :], [pre_done[-1]])

        v_store = []
        for vc in range(8):
            banks = [psb[3], psb[4], psb[5], psb[6]]
            bbank = [B_(("bank", 3 + i)) for i in range(4)]
            firstw = []
            for bb_ in bbank:
                firstw += bb_.wr()
            lasts = [None] * 4
            vtiles = rhsB_load_multi(wbf["wv"][vc].rearrange("p a b -> p (a b)"),
                                     16, [prolog["wv"]], f"wvl{tagg}_{vc}",
                                     q="gp", batch=4)
            for dc in range(16):
                sp_, slot, wl = vtiles[dc]
                for tc in range(4):
                    w = [wl] + ((xn_ld + firstw) if dc == 0 else [])
                    m = op("pe", lambda e, dc=dc, tc=tc, sp_=sp_, banks=banks: e.matmul(
                        banks[tc][:, :], xnv(dc, tc * 128, 128),
                        ringB[:, sp_ * 512:(sp_ + 1) * 512],
                        start=(dc == 0), stop=(dc == 15)), w,
                        f"vm{tagg}_{vc}_{dc}_{tc}")
                    slot.did_read(m)
                    lasts[tc] = m
            bvs = B_("vstage")
            evs = []
            for tc in range(4):
                bbank[tc].did_write(lasts[tc])
                ev = op("dve", lambda e, tc=tc, banks=banks: e.tensor_copy(
                    vstage[:, tc * 512:(tc + 1) * 512], banks[tc][:, :]),
                    [lasts[tc]] + (bvs.wr() if tc == 0 else []), f"vev{tagg}_{vc}_{tc}")
                bbank[tc].did_read(ev)
                evs.append(ev)
            bvs.did_write(evs[-1])
            so = dma("sp", vb[gstart:gstart + 512, vc * 512:(vc + 1) * 512]
                     .rearrange("(c p) f -> p c f", p=128),
                     vstage.rearrange("p (c f) -> p c f", c=4), [evs[-1]],
                     f"vst{tagg}_{vc}")
            bvs.did_read(so)
            v_store.append(so)

        bz3, bz4 = B_(("bank", 3)), B_(("bank", 4))
        mg = mb_ = None
        for dc in range(16):
            w = (init_ops + xn_ld + bz3.wr() + v_store) if dc == 0 else []
            mg = op("pe", lambda e, dc=dc: e.matmul(
                psb[3][0:16, :], wg_s[:, dc * 16:(dc + 1) * 16], xnv(dc, 0, 512),
                start=(dc == 0), stop=(dc == 15)), w, f"zg{tagg}_{dc}")
        for dc in range(16):
            w = (init_ops + xn_ld + bz4.wr()) if dc == 0 else []
            mb_ = op("pe", lambda e, dc=dc: e.matmul(
                psb[4][0:16, :], wb_s[:, dc * 16:(dc + 1) * 16], xnv(dc, 0, 512),
                start=(dc == 0), stop=(dc == 15)), w, f"zbm{tagg}_{dc}")
        bz3.did_write(mg)
        bz4.did_write(mb_)
        bzf = B_("zf")
        sgo = op("act", lambda e: e.activation(zg_f[:, :], psb[3][0:16, :], AF.Sigmoid,
                                               bias=bgc_s[:, 0:1]),
                 [mg] + bzf.wr(), f"sg{tagg}")
        sbo = op("act", lambda e: e.activation(zb_f[:, :], psb[4][0:16, :], AF.Sigmoid,
                                               bias=bbc_s[:, 0:1]), [mb_], f"sb{tagg}")
        bz3.did_read(sgo)
        bz4.did_read(sbo)
        lgo = op("act", lambda e: e.activation(lng_f[:, :], zg_f[:, :], AF.Ln),
                 [sgo], f"lg{tagg}")
        lbo = op("act", lambda e: e.activation(lnb_f[:, :], zb_f[:, :], AF.Ln),
                 [sbo], f"lbx{tagg}")
        bzf.did_write(lbo)
        z_done = [lgo, lbo]
        sch.set_join(p_start)

        for ci in range(4):
            cp = ci % 2
            tok0 = gstart + ci * 128
            bvc = B_(("vbuf", cp))
            vload = dma("sp", vbuf[:, cp * 4096:(cp + 1) * 4096],
                        vb[tok0:tok0 + 128, :], v_store + bvc.wr(), f"vld{tagg}_{ci}")
            bvc.did_write(vload)

            bl = B_("ladb")
            blb = B_("lbf")
            c0 = op("dve", lambda e, ci=ci: e.tensor_copy(
                lad[:, 128:256], lnb_f[:, ci * 128:(ci + 1) * 128]),
                z_done + bl.wr() + [zpad2], f"lc{tagg}_{ci}")
            cur = c0
            plan = [(128, 384, 1), (384, 128, 2), (128, 384, 4), (384, 128, 8),
                    (128, 384, 16), (384, 128, 32)]
            for st_i, (src0, dst0, sft) in enumerate(plan):
                cur = op("dve", lambda e, src0=src0, dst0=dst0, sft=sft: e.tensor_add(
                    lad[:, dst0:dst0 + 128], lad[:, src0:src0 + 128],
                    lad[:, src0 - sft:src0 + 128 - sft]), [cur],
                    f"ldd{tagg}_{ci}_{st_i}")
            lbchunk = op("dve", lambda e, ci=ci: e.tensor_add(
                lb_f[:, ci * 128:(ci + 1) * 128], lad[:, 128:256],
                lad[:, 64:192]), [cur] + blb.wr(), f"lbc{tagg}_{ci}")
            bl.did_write(lbchunk)
            blb.did_write(lbchunk)

            bdf = B_(("decf", cp))
            bdt0 = B_("dtmp")
            pd1 = op("dve", lambda e: e.tensor_copy(cstc[:, 3:4], cstc[:, 0:1]),
                     [], f"pad1{tagg}_{ci}")
            pd2 = op("dve", lambda e: e.tensor_copy(cstc[:, 3:4], cstc[:, 1:2]),
                     [pd1], f"pad2{tagg}_{ci}")
            d0 = op("dve", lambda e, ci=ci: e.tensor_scalar(
                dtmp[:, :], lb_f[:, ci * 128:(ci + 1) * 128],
                lb_f[:, ci * 128 + 127:ci * 128 + 128], None,
                ALU.subtract),
                bdt0.wr(), f"dc0{tagg}_{ci}", flush=[lbchunk], drain=True)
            d0 = op("dve", lambda e: e.tensor_scalar(
                dtmp[:, :], dtmp[:, :], -1.0, None, ALU.mult), [d0],
                f"dc0n{tagg}_{ci}")
            bdt0.did_write(d0)
            d1 = op("act", lambda e, cp=cp: e.activation(
                dec_f[:, cp * 128:(cp + 1) * 128], dtmp[:, :],
                AF.Exp), [d0] + bdf.wr(), f"dc1{tagg}_{ci}")
            bdf.did_write(d1)
            bdt0.did_read(d1)
            bac = B_(("aCc", cp))
            a0 = op("act", lambda e, ci=ci, cp=cp: e.activation(
                aC_c[:, cp:cp + 1], lb_f[:, ci * 128 + 127:ci * 128 + 128], AF.Exp),
                [lbchunk] + bac.wr(), f"ac{tagg}_{ci}")
            bac.did_write(a0)

            bt5 = B_(("bank", 5))
            blt = B_(("lbtm", cp))
            t1 = op("pe", lambda e, ci=ci: e.transpose(
                psb[5][:, 0:16], lb_f[:, ci * 128:(ci + 1) * 128],
                identf[0:16, 0:16]),
                [lbchunk] + bt5.wr() + init_ops, f"tlb{tagg}_{ci}")
            e1 = op("dve", lambda e, cp=cp: e.tensor_copy(
                lb_tm[:, cp * 16:(cp + 1) * 16], psb[5][:, 0:16]),
                [t1] + blt.wr(), f"tlbe{tagg}_{ci}")
            blt.did_write(e1)
            t2 = op("pe", lambda e, cp=cp: e.transpose(
                psb[5][:, 16:32], dec_f[:, cp * 128:(cp + 1) * 128],
                identf[0:16, 0:16]),
                [d1, e1], f"tdc{tagg}_{ci}")
            bt5.did_write(t2)
            bdt = B_(("dectm", cp))
            e2 = op("dve", lambda e, cp=cp: e.tensor_copy(
                dec_tm[:, cp * 16:(cp + 1) * 16], psb[5][:, 16:32]),
                [t2] + bdt.wr(), f"tdce{tagg}_{ci}")
            bdt.did_write(e2)
            bt5.did_read(e2)

            bps6 = B_(("bank", 6))
            alast = None
            for h in range(16):
                w = ([a0] + bps6.wr()) if h == 0 else []
                alast = op("pe", lambda e, h=h, cp=cp: e.matmul(
                    psb[6][:, h:h + 1], selb[:, h * 128:(h + 1) * 128],
                    aC_c[:, cp:cp + 1], start=True, stop=True), w,
                    f"acb{tagg}_{ci}_{h}")
            bps6.did_write(alast)
            bab = B_(("acb", cp))
            e3 = op("dve", lambda e, cp=cp: e.tensor_copy(
                acb[:, cp * 16:(cp + 1) * 16], psb[6][:, 0:16]),
                [alast] + bab.wr(), f"acbe{tagg}_{ci}")
            bab.did_write(e3)
            bps6.did_read(e3)

            if not is_halo:
                br1 = B_(("r1", cp))
                r0 = op("dve", lambda e, ci=ci: e.tensor_add(
                    r1_f[:, ci * 128:(ci + 1) * 128], lb_f[:, ci * 128:(ci + 1) * 128],
                    lng_f[:, ci * 128:(ci + 1) * 128]),
                    [lbchunk, lgo] + br1.wr(), f"r1{tagg}_{ci}")
                rh = op("dve", lambda e, ci=ci: e.tensor_copy(
                    r1h[:, ci * 128:(ci + 1) * 128], r1_f[:, ci * 128:(ci + 1) * 128]),
                    [r0], f"r1h{tagg}_{ci}")
                rl = op("dve", lambda e, ci=ci: e.scalar_tensor_tensor(
                    r1l[:, ci * 128:(ci + 1) * 128], r1h[:, ci * 128:(ci + 1) * 128],
                    -1.0, r1_f[:, ci * 128:(ci + 1) * 128], ALU.mult, ALU.add),
                    [rh], f"r1l{tagg}_{ci}")
                br1.did_write(rl)

            for h in range(16):
                hp = h % 2
                ks0 = h * 512 + ci * 128
                kslice = kg[:, ks0:ks0 + 128]
                if not is_halo:
                    qslice = qg[:, ks0:ks0 + 128]
                    bE = B_(("bank", 0))
                    eh1 = op("pe", lambda e, h=h, ci=ci: e.matmul(
                        psb[0][:, 0:128], selb[:, h * 128:(h + 1) * 128],
                        r1h[0:16, ci * 128:(ci + 1) * 128], start=True, stop=False),
                        [rl] + bE.wr(), f"E1{tagg}_{ci}_{h}")
                    eh2 = op("pe", lambda e, h=h, ci=ci: e.matmul(
                        psb[0][:, 0:128], selb[:, h * 128:(h + 1) * 128],
                        r1l[0:16, ci * 128:(ci + 1) * 128], start=False, stop=True),
                        [eh1], f"E2{tagg}_{ci}_{h}")
                    bE.did_write(eh2)
                    bA = B_(("bank", 1))
                    am = op("pe", lambda e, kslice=kslice, qslice=qslice: e.matmul(
                        psb[1][:, 0:128], kslice, qslice, start=True, stop=True),
                        [k_done[-1], q_done[-1]] + bA.wr(), f"Ar{tagg}_{ci}_{h}")
                    bA.did_write(am)
                    btD = B_(("tmpD", hp))
                    td = op("dve", lambda e, hp=hp, cp=cp, h=h: e.scalar_tensor_tensor(
                        tmpD[:, hp * 128:(hp + 1) * 128], psb[0][:, 0:128],
                        lb_tm[:, cp * 16 + h:cp * 16 + h + 1], cmask[:, :],
                        ALU.subtract, ALU.add), [eh2] + btD.wr() + init_ops,
                        f"tD{tagg}_{ci}_{h}", flush=[e1], drain=(h == 0))
                    btD.did_write(td)
                    bE.did_read(td)
                    blt.did_read(td)
                    bDp = B_(("Dp", hp))
                    dp = op("act", lambda e, hp=hp: e.activation(
                        Dp_s[:, hp * 128:(hp + 1) * 128],
                        tmpD[:, hp * 128:(hp + 1) * 128], AF.Exp),
                        [td] + bDp.wr(), f"Dp{tagg}_{ci}_{h}")
                    bDp.did_write(dp)
                    btD.did_read(dp)
                    bEq = B_(("Eq", hp))
                    eq = op("act", lambda e, hp=hp: e.activation(
                        Eq_s[:, hp * 128:(hp + 1) * 128], psb[0][:, 0:128], AF.Exp),
                        [eh2] + bEq.wr(), f"Eq{tagg}_{ci}_{h}")
                    bEq.did_write(eq)
                    bE.did_read(eq)
                    bAb = B_(("Abf", hp))
                    ab = op("dve", lambda e, hp=hp: e.tensor_mul(
                        A_bf[:, hp * 128:(hp + 1) * 128], psb[1][:, 0:128],
                        Dp_s[:, hp * 128:(hp + 1) * 128]),
                        [am, dp] + bAb.wr(), f"Ab{tagg}_{ci}_{h}")
                    bAb.did_write(ab)
                    bA.did_read(ab)
                    bDp.did_read(ab)
                    bQb = B_(("Qbf", hp))
                    qb = op("dve", lambda e, hp=hp, qslice=qslice: e.tensor_mul(
                        Qb_bf[:, hp * 128:(hp + 1) * 128], qslice,
                        Eq_s[:, hp * 128:(hp + 1) * 128]),
                        [eq] + bQb.wr(), f"Qb{tagg}_{ci}_{h}")
                    bQb.did_write(qb)
                    bEq.did_read(qb)
                    bSf = B_("Sfull")
                    bSb = B_("Sbfb")
                    scast = op("dve", lambda e, h=h: e.tensor_copy(
                        S_bf[:, :], S_fp[:, h * 256:(h + 1) * 256]),
                        bSb.wr(), f"Sc{tagg}_{ci}_{h}", flush=bSf.rd())
                    bSb.did_write(scast)
                    bSf.did_read(scast)
                    bO = B_(("bank", 2))
                    olast = None
                    ow = bO.wr()
                    for eh_ in range(2):
                        m1_ = op("pe", lambda e, eh_=eh_, hp=hp, h=h, cp=cp: e.matmul(
                            psb[2][:, eh_ * 128:(eh_ + 1) * 128],
                            vbuf[:, cp * 4096 + h * 256 + eh_ * 128:
                                 cp * 4096 + h * 256 + eh_ * 128 + 128],
                            A_bf[:, hp * 128:(hp + 1) * 128], start=True, stop=False),
                            [ab, vload] + ow, f"o1{tagg}_{ci}_{h}_{eh_}")
                        olast = op("pe", lambda e, eh_=eh_, hp=hp: e.matmul(
                            psb[2][:, eh_ * 128:(eh_ + 1) * 128],
                            S_bf[:, eh_ * 128:(eh_ + 1) * 128],
                            Qb_bf[:, hp * 128:(hp + 1) * 128], start=False, stop=True),
                            [qb, scast, m1_], f"o2{tagg}_{ci}_{h}_{eh_}")
                        ow = []
                    bO.did_write(olast)
                    bvc.did_read(olast)
                    bSb.did_read(olast)
                    bAb.did_read(olast)
                    bQb.did_read(olast)
                    ev1 = op("act", lambda e, h=h, ci=ci: e.copy(
                        o_grp[:, (2 * h) * 512 + ci * 128:
                              (2 * h) * 512 + ci * 128 + 128],
                        psb[2][:, 0:128]), [olast], f"oe1{tagg}_{ci}_{h}")
                    ev2 = op("act", lambda e, h=h, ci=ci: e.copy(
                        o_grp[:, (2 * h + 1) * 512 + ci * 128:
                              (2 * h + 1) * 512 + ci * 128 + 128],
                        psb[2][:, 128:256]), [olast], f"oe2{tagg}_{ci}_{h}")
                    bO.did_read(ev1)
                    bO.did_read(ev2)
                    B_("ogrp").did_read(ev2)

                bpt = B_(("bank", 7))
                kt = op("pe", lambda e, kslice=kslice: e.transpose(
                    psT[:, 0:128], kslice, ident[:, :]),
                    [k_done[-1]] + bpt.wr() + init_ops, f"Kt{tagg}_{ci}_{h}")
                bpt.did_write(kt)
                bKh = B_(("Khb", hp))
                kh = op("dve", lambda e, hp=hp, cp=cp, h=h: e.tensor_scalar(
                    Kh_bf[:, hp * 128:(hp + 1) * 128], psT[:, 0:128],
                    dec_tm[:, cp * 16 + h:cp * 16 + h + 1], None, ALU.mult),
                    [kt] + bKh.wr(), f"Kh{tagg}_{ci}_{h}", flush=[e2],
                    drain=(h == 0))
                bKh.did_write(kh)
                bpt.did_read(kh)
                bdt.did_read(kh)
                bSd = B_(("bank", 3))
                sd = op("pe", lambda e, hp=hp, cp=cp, h=h: e.matmul(
                    psb[3][:, 0:256], Kh_bf[:, hp * 128:(hp + 1) * 128],
                    vbuf[:, cp * 4096 + h * 256: cp * 4096 + (h + 1) * 256],
                    start=True, stop=True), [kh, vload] + bSd.wr(),
                    f"Sd{tagg}_{ci}_{h}")
                bSd.did_write(sd)
                bvc.did_read(sd)
                bKh.did_read(sd)
                bSf = B_("Sfull")
                su = op("dve", lambda e, h=h, cp=cp: e.scalar_tensor_tensor(
                    S_fp[:, h * 256:(h + 1) * 256], S_fp[:, h * 256:(h + 1) * 256],
                    acb[:, cp * 16 + h:cp * 16 + h + 1], psb[3][:, 0:256],
                    ALU.mult, ALU.add), [sd] + bSf.wr(), f"Su{tagg}_{ci}_{h}",
                    flush=[e3], drain=(h == 0))
                bSf.did_write(su)
                bSd.did_read(su)
                bab.did_read(su)

        if is_halo:
            sch.set_join(p_start)
            bSf = B_("Sfull")
            mo = op("dve", lambda e: e.tensor_scalar(
                S_fp[:, :], S_fp[:, :], hmk[:, 0:1], None, ALU.mult),
                bSf.wr() + init_ops, "hmaskS")
            bSf.did_write(mo)
            for tn_, tv_ in [("lbH", lb_f[:, :]), ("lnbH", lnb_f[:, :]),
                             ("decH", dec_f[:, :]), ("dectmH", dec_tm[:, :]),
                             ("acbH", acb[:, :]), ("KhH", Kh_bf[:, :]),
                             ("vbufH", vbuf[:, :]), ("zbH", zb_f[:, :])]:
                tap(tn_, tv_, [mo])
            t_ = tap("SH", S_fp[:, :], [mo])
            if t_ is not None:
                bSf.did_read(t_)
            return

        sch.set_join(p_start)
        if g == 0:
            t_ = tap("ogrp0", o_grp[:, :], [])
            if t_ is not None:
                B_("ogrp").did_read(t_)
        if g == 3:
            t_ = tap("Sfin", S_fp[:, :], [])
        for cc in range(4):
            banks = [psb[3], psb[4], psb[5], psb[6]]
            btr = [B_(("bank", 3 + i)) for i in range(4)]
            lasts = [None] * 4
            wo2 = wbf["wo"][cc].rearrange("p a b -> p (a b)")
            wtiles = []
            for oc in range(32):
                if oc % 4 == 0:
                    wtiles += rhsB_load_multi(
                        wo2[:, oc * 512:(oc + 4) * 512], 4, [P("wo")],
                        f"wol{tagg}_{cc}_{oc}", q="gp", batch=4)
                sp_, slot, wl = wtiles[oc]
                for tc in range(4):
                    w = [wl] + (btr[tc].wr() if oc == 0 else [])
                    m = op("pe", lambda e, oc=oc, tc=tc, sp_=sp_, banks=banks: e.matmul(
                        banks[tc][:, :],
                        o_grp[:, oc * 512 + tc * 128: oc * 512 + tc * 128 + 128],
                        ringB[:, sp_ * 512:(sp_ + 1) * 512],
                        start=(oc == 0), stop=(oc == 31)), w,
                        f"wom{tagg}_{cc}_{oc}_{tc}")
                    slot.did_read(m)
                    lasts[tc] = m
            for tc in range(4):
                btr[tc].did_write(lasts[tc])
                pv = qg_f32[:, (tc % 2) * 512:(tc % 2) * 512 + 512]
                xv2 = qg_f32[:, 1024 + (tc % 2) * 512:1024 + (tc % 2) * 512 + 512]
                bpv = B_(("pv", tc % 2))
                bxv = B_(("xv", tc % 2))
                row0 = 512 + g * 512 + tc * 128
                xl = dma("sp", xv2, xc[row0:row0 + 128, cc * 512:(cc + 1) * 512],
                         bxv.wr(), f"xl{tagg}_{cc}_{tc}")
                bxv.did_write(xl)
                ad = op("dve", lambda e, tc=tc, pv=pv, xv2=xv2, banks=banks:
                        e.tensor_add(pv, banks[tc][:, :], xv2),
                        [lasts[tc], xl] + bpv.wr(), f"pa{tagg}_{cc}_{tc}")
                btr[tc].did_read(ad)
                bxv.did_read(ad)
                bpv.did_write(ad)
                st_ = dma("sp", partial[g * 512 + tc * 128: g * 512 + tc * 128 + 128,
                                        cc * 512:(cc + 1) * 512], pv, [ad],
                          f"pst{tagg}_{cc}_{tc}", ded="partial")
                bpv.did_read(st_)
                B_("partial").did_read(st_)
        sch.set_join(p_start)

    emit_group(0, is_halo=True)
    for g in range(4):
        emit_group(g, is_halo=False)

    p1_end = len(sch.ops)
    sch.set_join(0)

    for gd in reversed(guards[p01_guards_start:]):
        gd.__exit__(None, None, None)
    del guards[p01_guards_start:]

    acc = sb("acc", [128, 8 * 2048], F32)
    ringA2 = sb("ringA2", [128, 96 * 128], BF16)
    RA2 = 96
    ringA2B = [B_(("rA2", i)) for i in range(RA2)]
    rpos["A2"] = 0

    def statA2_load(src_ap, nt, waits, tag):
        pump_cast(2)
        p = rpos["A2"]
        if p + nt > RA2:
            p = 0
        rpos["A2"] = p + nt
        slots = ringA2B[p:p + nt]
        w = list(waits)
        for s_ in slots:
            w += s_.wr()
        o = dma("gp", ringA2[:, p * 128:(p + nt) * 128], src_ap, w, tag)
        for s_ in slots:
            s_.did_write(o)
        return p, slots, o
    harena = sb("harena", [128, 11 * 1024], BF16)
    xn_h = sb("xnh", [128, 16 * 1024], BF16)
    sg_s = sb("sgs", [128, 512], F32)
    sc_s = sb("scs", [128, 8 * 8], F32)
    bi_s = sb("bis", [128, 8 * 8], F32)
    b2_s = sb("b2s", [128, 8 * 8], F32)
    mk_s = sb("mks", [128, 8 * 8], F32)
    mk2_s = sb("mk2s", [128, 8 * 8], F32)
    mm_s = sb("mmscol", [128, 8 * 8], F32)
    we_s = sb("wes", [128, 64], F32)
    pbuf = sb("pbuf", [128, 2048], F32)
    obuf = sb("obuf", [128, 2048], F32)
    _pcnt = [0]

    for half in range(2):
        h_start = len(sch.ops)
        htok = 512 + half * 1024

        bxh = B_("xnh")
        xn_ld2 = []
        for dc in range(16):
            o = dma("sp", xn_h[:, dc * 1024:(dc + 1) * 1024],
                    xnb[htok:htok + 1024, dc * 128:(dc + 1) * 128],
                    ln_wait(htok, htok + 1024) + (bxh.wr() if dc == 0 else []),
                    f"xnl2_{half}_{dc}", transpose=True)
            xn_ld2.append(o)
        bxh.did_write(xn_ld2[-1])

        def xnv2(dc, lo, n):
            return xn_h[:, dc * 1024 + lo: dc * 1024 + lo + n]

        sig_done = []
        for tt in range(8):
            brt = B_(("bank", 4))
            last = None
            for dc in range(16):
                w = (xn_ld2 + init_ops + brt.wr()) if dc == 0 else []
                last = op("pe", lambda e, dc=dc, tt=tt: e.matmul(
                    psb[4][:, 0:8], xnv2(dc, tt * 128, 128),
                    wgt_s[:, dc * 8:(dc + 1) * 8], start=(dc == 0), stop=(dc == 15)),
                    w, f"rt{half}_{tt}_{dc}")
            brt.did_write(last)
            q0 = op("act", lambda e, tt=tt: e.activation(
                sc_s[:, tt * 8:(tt + 1) * 8], psb[4][:, 0:8], AF.Sigmoid),
                [last] + B_(("rts", tt)).wr(), f"rts{half}_{tt}")
            brt.did_read(q0)
            B_(("rts", tt)).did_write(q0)
            sig_done.append(q0)

        def rstep(name, fn_make, waits_extra=(), drain=False):
            outs = []
            for tt in range(8):
                o_ = op("dve", fn_make(tt), list(waits_extra), f"{name}{half}_{tt}",
                        drain=(drain and tt == 0))
                outs.append(o_)
            return outs

        s1o = rstep("rb", lambda tt: (lambda e, tt=tt: e.tensor_add(
            bi_s[:, tt * 8:(tt + 1) * 8], sc_s[:, tt * 8:(tt + 1) * 8],
            gb_b[:, :])), sig_done + init_ops)
        s2o = rstep("rm1", lambda tt: (lambda e, tt=tt: e.reduce_max(
            mm_s[:, tt * 8:tt * 8 + 1], bi_s[:, tt * 8:(tt + 1) * 8], axis=AX.X)))
        s3o = rstep("rk1", lambda tt: (lambda e, tt=tt: e.tensor_scalar(
            mk_s[:, tt * 8:(tt + 1) * 8], bi_s[:, tt * 8:(tt + 1) * 8],
            mm_s[:, tt * 8:tt * 8 + 1], None, ALU.is_ge)), drain=True)
        s4o = rstep("rb2", lambda tt: (lambda e, tt=tt: e.scalar_tensor_tensor(
            b2_s[:, tt * 8:(tt + 1) * 8], mk_s[:, tt * 8:(tt + 1) * 8], NEG,
            bi_s[:, tt * 8:(tt + 1) * 8], ALU.mult, ALU.add)))
        s5o = rstep("rm2", lambda tt: (lambda e, tt=tt: e.reduce_max(
            mm_s[:, tt * 8 + 1:tt * 8 + 2], b2_s[:, tt * 8:(tt + 1) * 8],
            axis=AX.X)))
        s6o = rstep("rk2", lambda tt: (lambda e, tt=tt: e.tensor_scalar(
            mk2_s[:, tt * 8:(tt + 1) * 8], b2_s[:, tt * 8:(tt + 1) * 8],
            mm_s[:, tt * 8 + 1:tt * 8 + 2], None, ALU.is_ge)), drain=True)
        s7o = rstep("rks", lambda tt: (lambda e, tt=tt: e.tensor_add(
            mk_s[:, tt * 8:(tt + 1) * 8], mk_s[:, tt * 8:(tt + 1) * 8],
            mk2_s[:, tt * 8:(tt + 1) * 8])))
        s8o = rstep("rw", lambda tt: (lambda e, tt=tt: e.tensor_mul(
            mk2_s[:, tt * 8:(tt + 1) * 8], sc_s[:, tt * 8:(tt + 1) * 8],
            mk_s[:, tt * 8:(tt + 1) * 8])))
        s9o = rstep("rsum", lambda tt: (lambda e, tt=tt: e.reduce_sum(
            mm_s[:, tt * 8 + 2:tt * 8 + 3], mk2_s[:, tt * 8:(tt + 1) * 8],
            axis=AX.X)))
        sao = rstep("rrc", lambda tt: (lambda e, tt=tt: e.reciprocal(
            mm_s[:, tt * 8 + 3:tt * 8 + 4], mm_s[:, tt * 8 + 2:tt * 8 + 3])))
        bwe = B_("wes")
        rt_done = []
        for tt in range(8):
            o_ = op("dve", lambda e, tt=tt: e.tensor_scalar(
                we_s[:, tt * 8:(tt + 1) * 8], mk2_s[:, tt * 8:(tt + 1) * 8],
                mm_s[:, tt * 8 + 3:tt * 8 + 4], None, ALU.mult),
                (bwe.wr() if tt == 0 else []), f"rwe{half}_{tt}",
                flush=[sao[tt]], drain=(tt == 0))
            rt_done.append(o_)
        B_("wes").did_write(rt_done[-1])
        if half == 0:
            tap("rta", bi_s[:, 0:8], [s1o[0]])
            tap("rtb", mk_s[:, 0:8], [s7o[0]])
            tap("rtc", mk2_s[:, 0:8], [s8o[0]])
            tap("rtd", mm_s[:, 0:8], [sao[0]])


        def up_tile_pre(ld, t_lo, t_n, bank, bkey, tag):
            pslot, slots, wl = ld
            bps = B_(bkey)
            last = None
            for dc in range(16):
                w = ([wl] + xn_ld2 + bps.wr()) if dc == 0 else []
                last = op("pe", lambda e, dc=dc, pslot=pslot, bank=bank,
                          t_lo=t_lo, t_n=t_n: e.matmul(
                    bank[:, 0:t_n],
                    ringA2[:, (pslot + dc) * 128:(pslot + dc + 1) * 128],
                    xnv2(dc, t_lo, t_n), start=(dc == 0), stop=(dc == 15)),
                    w, f"{tag}_{dc}")
            for s_ in slots:
                s_.did_read(last)
            bps.did_write(last)
            return last

        def up_pair_pre(ldy, ldg, t_lo, t_n, hcol, tag):
            pi = _pcnt[0] % 2
            _pcnt[0] += 1
            yb, gb = (0, 1) if pi == 0 else (2, 3)
            myl = up_tile_pre(ldy, t_lo, t_n, psb[yb], ("bank", yb), tag + "y")
            mgl = up_tile_pre(ldg, t_lo, t_n, psb[gb], ("bank", gb), tag + "g")
            bsg = B_("sgs")
            sgv = sg_s[:, 0:t_n]
            sl = op("act", lambda e, t_n=t_n, gb=gb, sgv=sgv: e.activation(
                sgv, psb[gb][:, 0:t_n], AF.Silu),
                [mgl] + bsg.wr(), tag + "s")
            bsg.did_write(sl)
            B_(("bank", gb)).did_read(sl)
            bha = B_("harena")
            hv = op("dve", lambda e, hcol=hcol, t_n=t_n, yb=yb, sgv=sgv:
                    e.tensor_mul(
                harena[:, hcol: hcol + t_n], sgv, psb[yb][:, 0:t_n]),
                [myl, sl] + bha.wr(), tag + "h")
            bsg.did_read(hv)
            B_(("bank", yb)).did_read(hv)
            bha.did_write(hv)
            return hv

        def up_pair(apy, apg, wwait, t_lo, t_n, hcol, tag):
            ldy = statA_load(apy, 16, wwait, tag + "y_l")
            ldg = statA_load(apg, 16, wwait, tag + "g_l")
            return up_pair_pre(ldy, ldg, t_lo, t_n, hcol, tag)

        def down_block(cc, tiles, nhc, hcols, tts, mode, tag):
            banks = [psb[3], psb[4], psb[5], psb[6]]
            btr = [B_(("bank", 3 + i)) for i in range(4)]
            lasts = [None] * 4
            bha = B_("harena")
            for hcx in range(nhc):
                sp_, slot, wl = tiles[hcx]
                for tc in range(4):
                    w = [wl] + bha.rd() + (btr[tc].wr() if hcx == 0 else [])
                    m = op("pe", lambda e, sp_=sp_, tc=tc, hcx=hcx, nhc=nhc,
                           banks=banks, hcols=hcols: e.matmul(
                        banks[tc][:, :], harena[:, hcols[hcx] + tc * 128:
                                                hcols[hcx] + tc * 128 + 128],
                        ringB[:, sp_ * 512:(sp_ + 1) * 512],
                        start=(hcx == 0), stop=(hcx == nhc - 1)), w,
                        f"{tag}_{hcx}_{tc}")
                    slot.did_read(m)
                    bha.did_read(m)
                    lasts[tc] = m
            for tc in range(4):
                btr[tc].did_write(lasts[tc])
                tt = tts[tc]
                bac2 = B_(("accb", tt, cc))
                av = acc[:, tt * 2048 + cc * 512: tt * 2048 + cc * 512 + 512]
                if mode == "copy":
                    ev = op("dve", lambda e, av=av, tc=tc, banks=banks:
                            e.tensor_copy(av, banks[tc][:, :]),
                            [lasts[tc]] + bac2.wr(), f"{tag}_ev{tc}")
                elif mode == "add":
                    ev = op("dve", lambda e, av=av, tc=tc, banks=banks:
                            e.tensor_add(av, banks[tc][:, :], av),
                            [lasts[tc]] + bac2.wr(), f"{tag}_ev{tc}")
                else:
                    ev = op("dve", lambda e, av=av, tc=tc, tt=tt, ex=mode,
                            banks=banks: e.scalar_tensor_tensor(
                        av, banks[tc][:, :], we_s[:, tt * 8 + ex: tt * 8 + ex + 1],
                        av, ALU.mult, ALU.add),
                        [lasts[tc], rt_done[tt]] + bac2.wr(), f"{tag}_ev{tc}",
                        drain=(tc == 0))
                btr[tc].did_read(ev)
                bac2.did_write(ev)

        for tq in range(2):
            t_lo = tq * 512
            lds_s = {}

            def shu_ld(p_, tq=tq, lds_s=lds_s):
                lds_s[p_] = (
                    statA2_load(wbf["sgw"][p_].rearrange("p a b -> p (a b)"),
                                16, [P("sgw")], f"shuly{half}_{tq}_{p_}"),
                    statA2_load(wbf["sgw"][22 + p_].rearrange(
                        "p a b -> p (a b)"), 16, [P("sgw")],
                        f"shulg{half}_{tq}_{p_}"))

            shu_ld(0)
            shu_ld(1)
            for p_ in range(22):
                if p_ + 2 < 22:
                    shu_ld(p_ + 2)
                ldy, ldg = lds_s.pop(p_)
                up_pair_pre(ldy, ldg, t_lo, 512, p_ * 512,
                            f"shu{half}_{tq}_{p_}")
            for cc in range(4):
                tiles = rhsB_load_multi(wbf["sdw"][cc].rearrange(
                    "p a b -> p (a b)"), 11, [P("sdw")],
                    f"shdl{half}_{tq}_{cc}", q="gp", batch=4)
                down_block(cc, tiles, 11, [hx * 512 for hx in range(11)],
                           [tq * 4 + tc for tc in range(4)], "copy",
                           f"shd{half}_{tq}_{cc}a")
                tiles2 = rhsB_load_multi(wbf["sdw"][cc].rearrange(
                    "p a b -> p (a b)")[:, 11 * 512:], 11, [P("sdw")],
                    f"shdl2{half}_{tq}_{cc}", q="gp", batch=4)
                down_block(cc, tiles2, 11,
                           [(11 + hx) * 512 for hx in range(11)],
                           [tq * 4 + tc for tc in range(4)], "add",
                           f"shd{half}_{tq}_{cc}b")

        if half == 0 and "accsh" in DBG:
            acw_ = [B_(("accb", t2_, c2_)).w for t2_ in range(8)
                    for c2_ in range(4)]
            t_ = tap("accsh", acc[:, :], acw_)
            for t2_ in range(8):
                for c2_ in range(4):
                    B_(("accb", t2_, c2_)).did_read(t_)

        ex_src = [(ex, p_) for ex in range(8) for p_ in range(11)]
        ex_lds = {}

        def ex_ld(i):
            ex, p_ = ex_src[i]
            ex_lds[i] = (
                statA2_load(wbf["wup"][ex, 11 + p_].rearrange(
                    "p a b -> p (a b)"), 16, [P("wup")],
                    f"exuly{half}_{ex}_{p_}"),
                statA2_load(wbf["wup"][ex, p_].rearrange(
                    "p a b -> p (a b)"), 16, [P("wup")],
                    f"exulg{half}_{ex}_{p_}"))

        ex_ld(0)
        ex_ld(1)
        idx = 0
        for ex in range(8):
            for p_ in range(11):
                if idx + 2 < len(ex_src):
                    ex_ld(idx + 2)
                ldy, ldg = ex_lds.pop(idx)
                for t5 in range(2):
                    up_pair_pre(ldy, ldg, t5 * 512, 512,
                                p_ * 1024 + t5 * 512,
                                f"exu{half}_{ex}_{t5}_{p_}")
                idx += 1
            for cc in range(4):
                tiles = rhsB_load_multi(wbf["wdn"][ex, cc].rearrange(
                    "p a b -> p (a b)"), 11, [P("wdn")],
                    f"exdl{half}_{ex}_{cc}", q="gp", batch=4)
                for sub in range(2):
                    down_block(cc, tiles, 11,
                               [hx * 1024 + sub * 512 for hx in range(11)],
                               [sub * 4 + tc for tc in range(4)], ex,
                               f"exd{half}_{ex}_{cc}_{sub}")

        if half == 0:
            t_ = tap("wes0", we_s[:, :], [rt_done[-1]])
            acw = [B_(("accb", t2_, c2_)).w for t2_ in range(8) for c2_ in range(4)]
            t_ = tap("acc0", acc[:, :], acw)
            if t_ is not None:
                for t2_ in range(8):
                    for c2_ in range(4):
                        B_(("accb", t2_, c2_)).did_read(t_)

        for tt in range(8):
            bpb = B_("pbuf")
            bob = B_("obuf")
            pl = dma("sp", pbuf[:, :],
                     partial[half * 1024 + tt * 128: half * 1024 + tt * 128 + 128, :],
                     bpb.wr() + B_("partial").rd(), f"pl{half}_{tt}")
            bpb.did_write(pl)
            ac_waits = [B_(("accb", tt, cc)).w for cc in range(4)]
            ao = op("dve", lambda e, tt=tt: e.tensor_add(
                obuf[:, :], acc[:, tt * 2048:(tt + 1) * 2048], pbuf[:, :]),
                [pl] + ac_waits + bob.wr(), f"fin{half}_{tt}")
            bpb.did_read(ao)
            for cc in range(4):
                B_(("accb", tt, cc)).did_read(ao)
            bob.did_write(ao)
            so = dma("sp", out_d[half * 1024 + tt * 128: half * 1024 + tt * 128 + 128, :],
                     obuf[:, :], [ao], f"os{half}_{tt}", ded="outs")
            bob.did_read(so)
            B_("outd").did_read(so)
        sch.set_join(h_start)

    ftaps = [tap("xnb", xnb[:, :], ln_done), tap("vb", vb[:, :], []),
             tap("partial", partial[:, :], [])]
    if "xend" in DBG:
        dma("sp", dbg_outs["xend"], xc[0:128, :], [sch.ops[-1]], "tap_xend")

    fins = sch.final_ops()
    sch.add("pe", None, fins, "final_wait_pe")
    sch.add("dve", None, fins, "final_wait_dve")

    import contextlib
    with contextlib.ExitStack() as est:
        sems = {}
        for nm in sorted(sch.sem_names):
            sems[nm] = est.enter_context(nc.semaphore("s_" + nm))
        import kernel as _K
        _K.SEM_NUMS = {nm: s.num for nm, s in sems.items()}
        s_gate = est.enter_context(nc.semaphore("s_gate"))
        nums = sorted(s_.num for s_ in sems.values())
        srange = range(nums[0], nums[-1] + 1)
        M_ = 1 << 20
        nc.gpsimd.sem_clear(srange)
        nc.gpsimd.dma_reset(srange)
        nc.gpsimd.sem_clear(srange)
        nc.gpsimd.sem_inc(s_gate, M_)
        for eng_ in (nc.tensor, nc.vector, nc.scalar, nc.sync):
            eng_.wait_ge(s_gate, M_)
            eng_.sem_inc(s_gate, M_)
            eng_.wait_ge(s_gate, 6 * M_)
        nc.gpsimd.wait_ge(s_gate, 5 * M_)
        nc.gpsimd.sem_clear(srange)
        nc.gpsimd.sem_inc(s_gate, M_)
        block = nc.Block()
        block.__enter__()

        @block.tensor
        def _(e):
            sch.emit("pe", e, sems)

        @block.vector
        def _(e):
            sch.emit("dve", e, sems)

        @block.scalar
        def _(e):
            sch.emit(("act", "gp"), e, sems)

        @block.sync
        def _(e):
            sch.emit("sp", e, sems)

        @block.gpsimd
        def _(e):
            sch.emit("ac", e, sems)

        block.__exit__(None, None, None)
        nc.all_engine_barrier()
        nc.gpsimd.sem_clear(srange)
        nc.all_engine_barrier()

    for gd in reversed(guards):
        gd.__exit__(None, None, None)
    import kernel as _K
    _K.PE_TAGS = [o.tag for o in sch.ops if o.unit == "pe" and o.fn is not None]
    return nc


def kernel(**inputs):
    st = _stage_weights(inputs)
    x = np.asarray(inputs["L0_attn_hidden_states"], dtype=np.float32)
    nc = build_program()
    in_maps = []
    for c in range(8):
        b, q = divmod(c, 4)
        t0 = q * TQ
        lo = t0 - HALO
        if lo < 0:
            xcore = np.concatenate(
                [np.zeros((-lo, D), np.float32), x[b, 0:t0 + TQ]], axis=0)
        else:
            xcore = np.ascontiguousarray(x[b, lo:t0 + TQ])
        hm = np.full((128, 1), 0.0 if q == 0 else 1.0, np.float32)
        m = {"xc": xcore, "hmask": hm, "ident": st["ident"], "identf": st["identf"],
             "ones": st["ones"], "sel": st["sel"]}
        for nm, _ in _WSHAPES:
            m[nm] = st[nm]
        in_maps.append(m)
    res = run_bass_kernel_spmd(nc, in_maps, core_ids=list(range(8)))
    out = np.empty((B, S, D), np.float32)
    for c in range(8):
        b, q = divmod(c, 4)
        out[b, q * TQ:(q + 1) * TQ] = res.results[c]["out"]
    return out


# version 13 (final)
# speedup vs baseline: 1.2030x; 1.2030x over previous
import numpy as np
import ml_dtypes
import concourse.bass as bass
import concourse.mybir as mybir
from concourse.bass_utils import run_bass_kernel_spmd

F32 = mybir.dt.float32
BF16 = mybir.dt.bfloat16
AF = mybir.ActivationFunctionType
ALU = mybir.AluOpType
AX = mybir.AxisListType
BF = ml_dtypes.bfloat16

DBG = {}

B, S, D = 2, 8192, 2048
TQ = 2048
HALO = 512
XT = HALO + TQ
NEG = -1e30

_NSLOT = 40
_SEM_STEP = {"pe": 1, "dve": 1, "act": 1}


class _Op:
    __slots__ = ("unit", "sem", "fn", "waits", "cnt", "pos", "tag", "slot",
                 "drain")


class _Slot:
    __slots__ = ("name", "uses", "waiters", "last")

    def __init__(self, name):
        self.name = name
        self.uses = 0
        self.waiters = {}
        self.last = None


class Sched:
    def __init__(self):
        self.ops = []
        self._cnt = {k: 0 for k in _SEM_STEP}
        self.join = []
        self.slots = {"sp": [_Slot(f"sp{i}") for i in range(_NSLOT)],
                      "gp": [_Slot(f"gp{i}") for i in range(_NSLOT)]}
        self._rr = {"sp": 0, "gp": 0}
        self.ded = {}
        self.sem_names = set(_SEM_STEP) | {s.name for p in self.slots.values()
                                           for s in p}

    def _collect(self, waits, unit, flush=()):
        ws = []
        for w in waits:
            if w is None:
                continue
            if isinstance(w, (list, tuple)):
                ws.extend(x for x in w if x is not None)
            else:
                ws.append(w)
        ws.extend(self.join)
        best = {}
        for w in ws:
            if w.unit == unit:
                continue
            if w.sem not in best or w.cnt > best[w.sem].cnt:
                best[w.sem] = w
        for w in flush:
            if w is None:
                continue
            if w.sem not in best or w.cnt > best[w.sem].cnt:
                best[w.sem] = w
        return best

    def add(self, sem, fn, waits=(), tag="", flush=(), drain=False):
        assert sem in _SEM_STEP
        unit = sem
        best = self._collect(waits, unit, flush)
        op = _Op()
        op.unit, op.sem, op.fn, op.tag, op.slot = unit, sem, fn, tag, None
        op.drain = drain
        op.waits = list(best.values())
        op.pos = len(self.ops)
        if fn is not None:
            self._cnt[sem] += _SEM_STEP[sem]
        op.cnt = self._cnt[sem]
        for w in op.waits:
            assert w.pos < op.pos, f"wait on future op {w.tag} from {tag}"
            if w.slot is not None:
                sl = w.slot
                prev = sl.waiters.get(op.sem)
                if prev is None or op.cnt >= prev.cnt:
                    sl.waiters[op.sem] = op
        self.ops.append(op)
        return op

    def _add_dma_one(self, q, fn, best, tag, ded):
        op = _Op()
        op.unit, op.fn, op.tag = q, fn, tag
        op.pos = len(self.ops)
        if ded is not None:
            nm = "d_" + ded
            if nm not in self.ded:
                self.ded[nm] = [nm, 0]
                self.sem_names.add(nm)
            self.ded[nm][1] += 16
            op.sem = nm
            op.cnt = self.ded[nm][1]
            op.slot = None
        else:
            pool = self.slots[q]
            sl = pool[self._rr[q] % _NSLOT]
            self._rr[q] += 1
            for wop in sl.waiters.values():
                if wop.sem not in best or wop.cnt > best[wop.sem].cnt:
                    best[wop.sem] = wop
            sl.waiters = {}
            sl.uses += 1
            op.sem = sl.name
            op.cnt = 16 * sl.uses
            op.slot = sl
            sl.last = op
        op.waits = [w for w in best.values() if w is not op]
        for w in op.waits:
            assert w.pos < op.pos, f"wait on future op {w.tag} from {tag}"
            if w.slot is not None and w.slot is not op.slot:
                prev = w.slot.waiters.get("__q_" + op.sem)
                if prev is None or op.cnt >= prev.cnt:
                    w.slot.waiters["__q_" + op.sem] = op
        self.ops.append(op)
        return op

    def add_dma(self, q, fn, waits=(), tag="", ded=None, dup=False):
        assert q in ("sp", "gp") or (q == "ac" and ded is not None)
        best = self._collect(waits, q)
        op = self._add_dma_one(q, fn, best, tag, ded)
        if dup:
            if op.slot is not None:
                sl = op.slot
                self._rr[q] -= 1 if False else 0
                op2 = _Op()
                op2.unit, op2.fn, op2.tag = q, fn, tag + "_d"
                op2.pos = len(self.ops)
                sl.uses += 1
                op2.sem = sl.name
                op2.cnt = 16 * sl.uses
                op2.slot = sl
                op2.waits = []
                sl.last = op2
                self.ops.append(op2)
                op = op2
            else:
                op = self._add_dma_one(q, fn, {}, tag + "_d", ded)
        return op

    def set_join(self, since_pos):
        last = {}
        for opx in self.ops[since_pos:]:
            if opx.fn is not None:
                last[opx.sem] = opx
        for w in self.join:
            if w.sem not in last or w.cnt > last[w.sem].cnt:
                last[w.sem] = w
        self.join = list(last.values())

    def final_ops(self):
        last = {}
        for o_ in self.ops:
            if o_.fn is not None:
                last[o_.sem] = o_
        return list(last.values())

    def emit(self, units, eng, sems):
        if isinstance(units, str):
            units = (units,)
        waited = {}
        for op in self.ops:
            if op.unit not in units:
                continue
            for w in op.waits:
                if w.cnt > waited.get(w.sem, 0):
                    eng.wait_ge(sems[w.sem], w.cnt)
                    waited[w.sem] = w.cnt
            if op.fn is None:
                continue
            inst = op.fn(eng)
            assert inst is not None, f"op {op.tag} emitted nothing"
            if op.unit in ("sp", "gp", "ac"):
                inst.then_inc(sems[op.sem], 16)
            elif op.unit == "pe":
                inst.then_inc(sems[op.sem], 1)
            else:
                eng.drain()
                eng.sem_inc(sems[op.sem], 1)


class Buf:
    def __init__(self):
        self.w = None
        self.r = {}

    def rd(self):
        return [self.w] if self.w is not None else []

    def wr(self):
        out = list(self.r.values())
        if self.w is not None:
            out.append(self.w)
        return out

    def did_read(self, op):
        prev = self.r.get(op.sem)
        if prev is None or op.cnt > prev.cnt:
            self.r[op.sem] = op

    def did_write(self, op):
        self.w = op
        self.r = {}


def _stage_weights(inp):
    st = {}

    def c(a):
        return np.ascontiguousarray(a, dtype=np.float32)

    st["wq"] = c(inp["L0_attn_Wq"].reshape(16, 128, 16, 128).transpose(0, 3, 2, 1))
    st["wk"] = c(inp["L0_attn_Wk"].reshape(16, 128, 16, 128).transpose(0, 3, 2, 1))
    st["wv"] = c(inp["L0_attn_Wv"].reshape(8, 512, 16, 128).transpose(0, 3, 2, 1))
    st["wo"] = c(inp["L0_attn_Wo"].reshape(4, 512, 32, 128).transpose(0, 3, 2, 1))
    st["wcq"] = c(inp["L0_attn_Wq_conv"].reshape(16, 128, 128, 4).transpose(0, 2, 3, 1))
    st["wck"] = c(inp["L0_attn_Wk_conv"].reshape(16, 128, 128, 4).transpose(0, 2, 3, 1))
    st["bcq"] = c(inp["L0_attn_bq_conv"].reshape(16, 128).T)
    st["bck"] = c(inp["L0_attn_bk_conv"].reshape(16, 128).T)
    st["wg"] = c(inp["L0_attn_Wg"].T.reshape(16, 128, 16).transpose(1, 0, 2))
    st["wb"] = c(inp["L0_attn_Wb"].T.reshape(16, 128, 16).transpose(1, 0, 2))
    st["bgc"] = c(inp["L0_attn_bg"].reshape(16, 1))
    st["bbc"] = c(inp["L0_attn_bb"].reshape(16, 1))
    st["wgate"] = c(inp["L0_ffn_gate_w"].T.reshape(16, 128, 8).transpose(1, 0, 2))
    st["gbias"] = c(inp["L0_ffn_gate_bias"].reshape(1, 8))
    st["sgw"] = c(inp["L0_ffn_shared_gw"].reshape(44, 128, 16, 128).transpose(0, 3, 2, 1))
    st["sdw"] = c(inp["L0_ffn_shared_dw"].reshape(4, 512, 22, 128).transpose(0, 3, 2, 1))
    st["wup"] = c(inp["L0_ffn_up_w"].reshape(8, 22, 128, 16, 128).transpose(0, 1, 4, 3, 2))
    st["wdn"] = c(inp["L0_ffn_down_w"].reshape(8, 4, 512, 11, 128)
                  .transpose(0, 1, 4, 3, 2))
    st["cmask"] = np.where(np.arange(128)[:, None] <= np.arange(128)[None, :],
                           np.float32(0.0), np.float32(NEG)).astype(np.float32)
    st["ident"] = np.eye(128, dtype=BF)
    st["identf"] = np.eye(128, dtype=np.float32)
    st["ones"] = np.ones((1, 128), dtype=BF)
    sel = np.zeros((16, 16 * 128), dtype=BF)
    for h in range(16):
        sel[h, h * 128:(h + 1) * 128] = 1.0
    st["sel"] = sel
    return st


_WSHAPES = [
    ("wq", [16, 128, 16, 128]), ("wk", [16, 128, 16, 128]),
    ("wv", [8, 128, 16, 512]), ("wo", [4, 128, 32, 512]),
    ("wcq", [16, 128, 4, 128]), ("wck", [16, 128, 4, 128]),
    ("bcq", [128, 16]), ("bck", [128, 16]),
    ("wg", [128, 16, 16]), ("wb", [128, 16, 16]),
    ("bgc", [16, 1]), ("bbc", [16, 1]),
    ("wgate", [128, 16, 8]), ("gbias", [1, 8]),
    ("sgw", [44, 128, 16, 128]), ("sdw", [4, 128, 22, 512]),
    ("wup", [8, 22, 128, 16, 128]), ("wdn", [8, 4, 128, 11, 512]),
    ("cmask", [128, 128]),
]


def build_program():
    nc = bass.Bass("TRN2", target_bir_lowering=False, debug=False)
    sch = Sched()

    xc = nc.dram_tensor("xc", [XT, D], F32, kind="ExternalInput").ap()
    hmask_d = nc.dram_tensor("hmask", [128, 1], F32, kind="ExternalInput").ap()
    din = {}
    for nm, shp in _WSHAPES:
        din[nm] = nc.dram_tensor(nm, shp, F32, kind="ExternalInput").ap()
    ident_d = nc.dram_tensor("ident", [128, 128], BF16, kind="ExternalInput").ap()
    identf_d = nc.dram_tensor("identf", [128, 128], F32, kind="ExternalInput").ap()
    ones_d = nc.dram_tensor("ones", [1, 128], BF16, kind="ExternalInput").ap()
    sel_d = nc.dram_tensor("sel", [16, 16 * 128], BF16, kind="ExternalInput").ap()
    out_d = nc.dram_tensor("out", [TQ, D], F32, kind="ExternalOutput").ap()
    dbg_outs = {}
    _dbg_shapes = {"xnb": ([XT, D], BF16), "vb": ([XT, 4096], BF16),
                   "partial": ([TQ, D], F32), "SH": ([128, 16 * 256], F32),
                   "qg0": ([128, 16 * 512], BF16), "kg0": ([128, 16 * 512], BF16),
                   "ogrp0": ([128, 32 * 512], BF16), "acc0": ([128, 8 * 2048], F32),
                   "wes0": ([128, 64], F32), "xng0": ([128, 16 * 512], BF16),
                   "preq0": ([128, 16480], BF16), "Sfin": ([128, 16 * 256], F32),
                   "gbb": ([128, 8], F32), "rta": ([128, 8], F32),
                   "lbH": ([16, 512], F32), "lnbH": ([16, 512], F32),
                   "decH": ([16, 256], F32), "dectmH": ([128, 32], F32),
                   "acbH": ([128, 32], F32), "KhH": ([128, 256], BF16),
                   "vbufH": ([128, 8192], BF16), "zbH": ([16, 512], F32),
                   "accsh": ([128, 8 * 2048], F32),
                   "rtb": ([128, 8], F32), "rtc": ([128, 8], F32),
                   "rtd": ([128, 8], F32), "xstart": ([128, 2048], F32),
                   "xend": ([128, 2048], F32), "snapx": ([128, 2048], F32),
                   "stat0": ([128, 16], F32)}
    for nm_ in DBG:
        shp_, dt_ = _dbg_shapes[nm_]
        dbg_outs[nm_] = nc.dram_tensor("dbg_" + nm_, shp_, dt_,
                                       kind="ExternalOutput").ap()

    xnb = nc.dram_tensor("xnb", [XT, D], BF16, kind="Internal").ap()
    vb = nc.dram_tensor("vbd", [XT, 4096], BF16, kind="Internal").ap()
    partial = nc.dram_tensor("partiald", [TQ, D], F32, kind="Internal").ap()
    wbf = {}
    for nm in ["wq", "wk", "wv", "wo", "wcq", "wck", "sgw", "sdw", "wup", "wdn"]:
        wbf[nm] = nc.dram_tensor(nm + "_b", list(din[nm].shape), BF16,
                                 kind="Internal").ap()

    guards = []

    def sb(name, shape, dtype):
        g = nc.sbuf_tensor(name, shape, dtype)
        t = g.__enter__()
        guards.append(g)
        return t

    def psa(name, shape, dtype=F32):
        g = nc.psum_tensor(name, shape, dtype)
        t = g.__enter__()
        guards.append(g)
        return t

    ident = sb("identsb", [128, 128], BF16)
    identf = sb("identfsb", [128, 128], F32)
    onesb = sb("onessb", [1, 128], BF16)
    selb = sb("selsb", [16, 16 * 128], BF16)
    cmask = sb("cmasksb", [128, 128], F32)
    hmk = sb("hmksb", [128, 1], F32)
    bcq_s = sb("bcqs", [128, 16], F32)
    bck_s = sb("bcks", [128, 16], F32)
    bgc_s = sb("bgcs", [16, 1], F32)
    bbc_s = sb("bbcs", [16, 1], F32)
    wg_s = sb("wgs", [128, 16 * 16], BF16)
    wb_s = sb("wbs", [128, 16 * 16], BF16)
    wgt_s = sb("wgts", [128, 16 * 8], BF16)
    gb_row = sb("gbrow", [1, 8], F32)
    gb_hi = sb("gbhi", [1, 8], BF16)
    gb_lo = sb("gblo", [1, 8], BF16)
    gb_b = sb("gbb", [128, 8], F32)
    gb_pad = sb("gbpad", [16, 8], F32)
    gb_padb = sb("gbpadb", [16, 8], BF16)
    ones16 = sb("ones16", [16, 128], BF16)
    cstc = sb("cstc", [128, 4], F32)
    sv_q = sb("svq", [128, 48], BF16)
    sv_k = sb("svk", [128, 48], BF16)
    ringB = sb("ringBs", [128, 20 * 512], BF16)
    RB = 20

    psb = [psa(f"psb{i}", [128, 512], F32) for i in range(7)]
    psT = psa("psTb", [128, 512], BF16)

    bufs = {}

    def B_(key):
        if key not in bufs:
            bufs[key] = Buf()
        return bufs[key]

    ringAB = [B_(("rA", i)) for i in range(64)]
    ringBB = [B_(("rB", i)) for i in range(RB)]
    rpos = {"A": 0, "B": 0}

    def tap(nm_, src_ap, waits):
        if nm_ not in DBG:
            return None
        return dma("sp", dbg_outs[nm_], src_ap, waits, "tap_" + nm_)

    def dma(q, out, in_, waits=(), tag="", dup=False, ded=None, **kw):
        def fn(e, out=out, in_=in_, kw=kw):
            return e.dma_start(out=out, in_=in_, **kw)
        return sch.add_dma(q, fn, waits, tag, ded=ded, dup=dup)

    op = sch.add

    if "xstart" in DBG:
        dma("sp", dbg_outs["xstart"], xc[0:128, :], (), "tap_xstart")
    init_ops = []
    for dst, src in [(ident, ident_d), (identf, identf_d), (onesb, ones_d),
                     (selb, sel_d), (cmask, din["cmask"]), (hmk, hmask_d),
                     (bcq_s, din["bcq"]), (bck_s, din["bck"]),
                     (bgc_s, din["bgc"]), (bbc_s, din["bbc"]),
                     (gb_row, din["gbias"])]:
        o = dma("sp", dst[tuple(slice(None) for _ in dst.shape)], src, (), "init",
                ded="init")
        init_ops.append(o)
    for dst, src in [(wg_s, din["wg"].rearrange("p a b -> p (a b)")),
                     (wb_s, din["wb"].rearrange("p a b -> p (a b)")),
                     (wgt_s, din["wgate"].rearrange("p a b -> p (a b)"))]:
        o = dma("ac", dst[:, :], src, (), "initw", ded="initw")
        init_ops.append(o)

    c0_ = op("dve", lambda e: e.memset(cstc[:, 0:1], 1e-5), (), "cst0")
    c1_ = op("dve", lambda e: e.memset(cstc[:, 1:2], -1.0), [c0_], "cst1")
    w0_ = op("act", lambda e: e.activation(cstc[:, 2:3], cstc[:, 0:1], AF.Square),
             [c1_], "warm_sq")
    w1_ = op("act", lambda e: e.activation(cstc[:, 2:3], cstc[:, 0:1], AF.Exp),
             [w0_], "warm_exp")
    w2_ = op("act", lambda e: e.activation(cstc[:, 2:3], cstc[:, 0:1], AF.Silu),
             [w1_], "warm_silu")
    w3_ = op("act", lambda e: e.activation(cstc[:, 3:4], cstc[:, 0:1], AF.Square,
                                           accum_out=cstc[:, 2:3]), [w2_], "warm_acc")
    init_ops = init_ops + [c1_, w3_]
    gb16 = gb_hi
    o16 = op("dve", lambda e: e.memset(ones16[:, :], 1.0), (), "ones16i")
    o3a = op("dve", lambda e: e.memset(gb_pad[:, :], 0.0), init_ops, "gbz")
    o3b = dma("sp", gb_pad[0:1, 0:8], din["gbias"], init_ops + [o3a], "gbld",
              ded="init")
    o3c = op("dve", lambda e: e.tensor_copy(gb_padb[:, :], gb_pad[:, :]),
             [o3a, o3b], "gbcast")
    m1_ = op("pe", lambda e: e.matmul(psb[0][:, 0:8], ones16[:, :],
                                      gb_padb[:, :], start=True, stop=True),
             [o3c, o16] + init_ops + B_(("bank", 0)).wr(), "gbmm")
    B_(("bank", 0)).did_write(m1_)
    o3 = op("dve", lambda e: e.tensor_copy(gb_b[:, :], psb[0][:, 0:8]),
            [m1_], "gbev")
    B_(("bank", 0)).did_read(o3)
    init_ops = init_ops + [o3]
    tap("gbb", gb_b[:, :], [o3])

    prolog = {}
    _castq = []

    def _chunks(nm):
        csrc, cdst = din[nm], wbf[nm]
        n0 = csrc.shape[0]
        pat = {3: "a b c -> a (b c)",
               4: "a b c d -> a (b c d)",
               5: "a b c d e -> a (b c d e)"}[len(csrc.shape)]
        fs = csrc.rearrange(pat)
        fd = cdst.rearrange(pat)
        X = fs.shape[1]
        step = max(1, min(X, (1 << 18)))
        out = []
        for r in range(n0):
            i = 0
            while i < X:
                j = min(X, i + step)
                out.append((nm, fd[r:r + 1, i:j], fs[r:r + 1, i:j], False))
                i = j
        nm_, a_, b_, _ = out[-1]
        out[-1] = (nm_, a_, b_, True)
        return out

    def cast_eager(names):
        for nm in names:
            for nm_, fd_, fs_, last_ in _chunks(nm):
                o = dma("ac", fd_, fs_, (), f"cast_{nm_}", ded="cast_" + nm_)
                if last_:
                    prolog[nm_] = o

    def cast_lazy(names):
        for nm in names:
            _castq.extend(_chunks(nm))

    def pump_cast(k):
        while k > 0 and _castq:
            nm_, fd_, fs_, last_ = _castq.pop(0)
            o = dma("ac", fd_, fs_, (), f"cast_{nm_}", ded="cast_" + nm_)
            if last_:
                prolog[nm_] = o
            k -= 1

    def P(nm):
        while nm not in prolog:
            pump_cast(16)
        return prolog[nm]

    cast_eager(["wk", "wck", "wq", "wcq", "wv"])
    cast_lazy(["wo", "sgw", "sdw", "wup", "wdn"])

    x_t = sb("xt", [128, 2048], F32)
    xn_t = sb("xnt", [128, 2 * 2048], BF16)
    sqs = sb("sqs", [128, 256], F32)
    stat = sb("stat", [128, 16], F32)
    snapx_b = sb("snapxb", [128, 2048], F32) if "snapx" in DBG else None
    stat0_b = sb("stat0b", [128, 16], F32) if "stat0" in DBG else None

    ln_done = []
    prev_tail = []
    for t in range(20):
        par = t % 2
        xv = x_t[:, :]
        xnv = xn_t[:, par * 2048:(par + 1) * 2048]
        bx, bxn = B_("xt"), B_(("xnt", par))
        lo = dma("sp", xv, xc[t * 128:(t + 1) * 128, :], bx.wr() + prev_tail,
                 f"ldx{t}")
        bx.did_write(lo)
        if t == 0 and snapx_b is not None:
            sxp = op("dve", lambda e: e.tensor_copy(snapx_b[:, :], x_t[:, :]),
                     [lo], "snapx0")
            bx.did_read(sxp)
            tap("snapx", snapx_b[:, :], [sxp])
        s0 = op("dve", lambda e: e.reduce_sum(stat[:, 0:1], x_t[:, :], axis=AX.X),
                [lo] + prev_tail, f"lsum{t}")
        sq_ops = []
        for j in range(8):
            so = op("act", lambda e, j=j: e.activation(
                sqs[:, :], x_t[:, j * 256:(j + 1) * 256], AF.Square,
                accum_out=stat[:, 8 + j:9 + j]),
                [lo, s0] + (prev_tail if j == 0 else []), f"lsq{t}_{j}")
            sq_ops.append(so)
        wa1 = op("act", lambda e: e.copy(cstc[:, 3:4], cstc[:, 0:1]),
                 [sq_ops[-1]], f"lwa1{t}")
        wa2 = op("act", lambda e: e.copy(cstc[:, 3:4], cstc[:, 1:2]),
                 [wa1], f"lwa2{t}")
        s1 = op("dve", lambda e: e.reduce_sum(stat[:, 1:2], stat[:, 8:16], axis=AX.X),
                [wa2], f"lsq2{t}")
        s2 = op("dve", lambda e: e.tensor_scalar(
            stat[:, 2:3], stat[:, 0:1], 1.0 / D, None, ALU.mult), [s0], f"lmean{t}")
        s3 = op("dve", lambda e: e.tensor_scalar(
            stat[:, 3:4], stat[:, 1:2], 1.0 / D, None, ALU.mult), [s1], f"lm2{t}")
        s4 = op("dve", lambda e: e.scalar_tensor_tensor(
            stat[:, 4:5], stat[:, 2:3], -1.0, stat[:, 2:3], ALU.mult, ALU.mult),
            [s2], f"lmm{t}")
        s5 = op("dve", lambda e: e.tensor_add(stat[:, 5:6], stat[:, 3:4], stat[:, 4:5]),
                [s3, s4], f"lvar{t}")
        s6 = op("act", lambda e: e.activation(stat[:, 6:7], stat[:, 5:6], AF.Sqrt,
                                              bias=cstc[:, 0:1]), [s5] + [c1_], f"lsd{t}")
        s7 = op("dve", lambda e: e.reciprocal(stat[:, 7:8], stat[:, 6:7]), [s6], f"lrs{t}")
        if t == 0 and stat0_b is not None:
            stp = op("dve", lambda e: e.tensor_copy(stat0_b[:, :], stat[:, :]),
                     [s7], "stat0c")
            tap("stat0", stat0_b[:, :], [stp])
        p1 = op("dve", lambda e: e.tensor_copy(cstc[:, 3:4], cstc[:, 0:1]),
                [], f"lpad1{t}")
        p2 = op("dve", lambda e: e.tensor_copy(cstc[:, 3:4], cstc[:, 1:2]),
                [p1], f"lpad2{t}")
        s8 = op("dve", lambda e, xnv=xnv: e.tensor_scalar(
            xnv, x_t[:, :], stat[:, 2:3], stat[:, 7:8], ALU.subtract, ALU.mult),
            bxn.wr(), f"lap{t}", flush=[s7], drain=True)
        bxn.did_write(s8)
        bx.did_read(s8)
        st_ = dma("sp", xnb[t * 128:(t + 1) * 128, :], xnv, [s8], f"sxn{t}")
        bxn.did_read(st_)
        ln_done.append(st_)
        prev_tail = [s6, s7, s8, sq_ops[-1], s1]

    def ln_wait(t0, t1):
        return [ln_done[t] for t in range(t0 // 128, min(20, (t1 + 127) // 128))]

    p01_guards_start = len(guards)
    ringA = sb("ringAs", [128, 64 * 128], BF16)
    RA = 64
    arena = sb("arena", [128, 16480], BF16)
    pre_q = arena[:, 0:8240]
    pre_k = arena[:, 8240:16480]
    o_grp = arena[:, 0:16384]
    xn_g = sb("xng", [128, 16 * 512], BF16)
    qg = sb("qg", [128, 16 * 512], BF16)
    kg = sb("kg", [128, 16 * 512], BF16)
    vbuf = sb("vbuf", [128, 2 * 4096], BF16)
    vstage = sb("vstage", [128, 4 * 512], BF16)
    S_fp = sb("Sfp", [128, 16 * 256], F32)
    S_bf = sb("Sbf", [128, 256], BF16)
    zg_f = sb("zgf", [16, 512], F32)
    zb_f = sb("zbf", [16, 512], F32)
    lng_f = sb("lngf", [16, 512], F32)
    lnb_f = sb("lnbf", [16, 512], F32)
    lb_f = sb("lbf", [16, 512], F32)
    r1_f = sb("r1f", [16, 512], F32)
    r1h = sb("r1h", [16, 512], BF16)
    r1l = sb("r1l", [16, 512], BF16)
    lad = sb("lad", [16, 512], F32)
    dec_f = sb("decf", [16, 2 * 128], F32)
    dtmp = sb("dtmpf", [16, 128], F32)
    aC_c = sb("aCc", [16, 2], BF16)
    lb_tm = sb("lbtm", [128, 2 * 16], F32)
    dec_tm = sb("dectm", [128, 2 * 16], F32)
    acb = sb("acbs", [128, 2 * 16], F32)
    tmpD = sb("tmpD", [128, 2 * 128], F32)
    Dp_s = sb("Dps", [128, 2 * 128], F32)
    Eq_s = sb("Eqs", [128, 2 * 128], F32)
    A_bf = sb("Abf", [128, 2 * 128], BF16)
    Qb_bf = sb("Qbbf", [128, 2 * 128], BF16)
    Kh_bf = sb("Khbf", [128, 2 * 128], BF16)
    qg_f32 = qg.bitcast(F32)

    zpad = op("dve", lambda e: e.memset(lad[:, 0:128], 0.0), (), "zlad0")
    zpad2 = op("dve", lambda e: e.memset(lad[:, 256:384], 0.0), [zpad], "zlad1")
    zS = op("dve", lambda e: e.memset(S_fp[:, :], 0.0), [zpad2], "zS")
    B_("Sfull").did_write(zS)

    def statA_load(src_ap, nt, waits, tag):
        pump_cast(2)
        p = rpos["A"]
        if p + nt > RA:
            p = 0
        rpos["A"] = p + nt
        slots = ringAB[p:p + nt]
        w = list(waits)
        for s_ in slots:
            w += s_.wr()
        o = dma("gp", ringA[:, p * 128:(p + nt) * 128], src_ap, w, tag)
        for s_ in slots:
            s_.did_write(o)
        return p, slots, o

    def rhsB_load_multi(src2_ap, n, waits, tag, q="sp", batch=4):
        out = []
        i = 0
        while i < n:
            pump_cast(2)
            j = min(n, i + batch)
            if rpos["B"] % RB + (j - i) > RB:
                rpos["B"] += RB - rpos["B"] % RB
            p = rpos["B"] % RB
            rpos["B"] += j - i
            slots = ringBB[p:p + (j - i)]
            w = list(waits)
            for s_ in slots:
                w += s_.wr()
            o = dma(q, ringB[:, p * 512:(p + j - i) * 512],
                    src2_ap[:, i * 512:j * 512], w, f"{tag}_{i}")
            for k_, s_ in enumerate(slots):
                s_.did_write(o)
                out.append((p + k_, s_, o))
            i = j
        return out

    def rhsB_load(src_ap, waits, tag, q="sp"):
        p = rpos["B"] % RB
        rpos["B"] = p + 1
        slot = ringBB[p]
        o = dma(q, ringB[:, p * 512:(p + 1) * 512], src_ap,
                list(waits) + slot.wr(), tag)
        slot.did_write(o)
        return p, slot, o

    def emit_group(g, is_halo):
        gstart = 0 if is_halo else 512 + 512 * g
        tagg = "H" if is_halo else f"G{g}"
        p_start = len(sch.ops)

        def xnv(dc, lo, n):
            return xn_g[:, dc * 512 + lo: dc * 512 + lo + n]

        xn_ld = []
        bxg = B_("xng")
        for dc in range(16):
            o = dma("sp", xn_g[:, dc * 512:(dc + 1) * 512],
                    xnb[gstart:gstart + 512, dc * 128:(dc + 1) * 128],
                    ln_wait(gstart, gstart + 512) + (bxg.wr() if dc == 0 else []),
                    f"xnl{tagg}_{dc}", transpose=True)
            xn_ld.append(o)
        bxg.did_write(xn_ld[-1])

        def proj_pre(which, wname, pre_view, copy_pads):
            pb = B_((which, "pre"))
            sv = sv_q if which == "q" else sv_k
            bsv = B_(("sv", which))
            pads = []
            if copy_pads:
                o = op("dve", lambda e, pre_view=pre_view, sv=sv: e.tensor_copy(
                    pre_view.rearrange("p (oc c) -> p oc c", c=515)[:, :, 0:3],
                    sv.rearrange("p (oc c) -> p oc c", c=3)[:, :, 0:3]),
                    pb.rd() + pb.wr() + bsv.rd(), f"pad{which}{tagg}")
                bsv.did_read(o)
                pads.append(o)
            evs = []
            lds_p = {}

            def _pld(oc2):
                if oc2 < 16:
                    lds_p[oc2] = statA_load(
                        wbf[wname][oc2].rearrange("p a b -> p (a b)"), 16,
                        [prolog[wname]], f"{wname}l{tagg}_{oc2}")

            _pld(0)
            _pld(1)
            for oc in range(16):
                _pld(oc + 2)
                pslot, slots, wl = lds_p.pop(oc)
                bank = psb[oc % 2]
                bps = B_(("bank", oc % 2))
                last = None
                for dc in range(16):
                    w = ([wl] + xn_ld + bps.wr() + pads) if dc == 0 else []
                    last = op("pe", lambda e, dc=dc, pslot=pslot, bank=bank: e.matmul(
                        bank[:, :], ringA[:, (pslot + dc) * 128:(pslot + dc + 1) * 128],
                        xnv(dc, 0, 512), start=(dc == 0), stop=(dc == 15)),
                        w, f"pj{which}{tagg}_{oc}_{dc}")
                for s_ in slots:
                    s_.did_read(last)
                bps.did_write(last)
                ev = op("act", lambda e, oc=oc, bank=bank, pre_view=pre_view: e.copy(
                    pre_view[:, oc * 515 + 3: oc * 515 + 515], bank[:, :]),
                    [last] + pads + (pb.wr() if oc == 0 else []),
                    f"pev{which}{tagg}_{oc}")
                bps.did_read(ev)
                evs.append(ev)
            pb.did_write(evs[-1])
            so_ = op("dve", lambda e, pre_view=pre_view, sv=sv: e.tensor_copy(
                sv.rearrange("p (oc c) -> p oc c", c=3)[:, :, 0:3],
                pre_view.rearrange("p (oc c) -> p oc c", c=515)[:, :, 512:515]),
                [evs[-1]] + bsv.wr(), f"sv{which}{tagg}")
            bsv.did_write(so_)
            pb.did_read(so_)
            evs.append(so_)
            return evs

        if is_halo:
            pbk = B_(("k", "pre"))
            mz = op("dve", lambda e: e.memset(
                pre_k.rearrange("p (oc c) -> p oc c", c=515)[:, :, 0:3], 0.0),
                pbk.wr() + [zS], "mspadk")
            pbk.did_write(mz)
            pre_done = proj_pre("k", "wk", pre_k, copy_pads=False)
            pbq = B_(("q", "pre"))
            lds_q = {}

            def _qld(oc2):
                if oc2 < 16:
                    lds_q[oc2] = statA_load(
                        wbf["wq"][oc2].rearrange("p a b -> p (a b)"), 16,
                        [prolog["wq"]], f"wqpl_{oc2}")

            _qld(0)
            _qld(1)
            for oc in range(16):
                _qld(oc + 2)
                pslot, slots, wl = lds_q.pop(oc)
                bank = psb[2]
                bps = B_(("bank", 2))
                last = None
                for dc in range(16):
                    w = ([wl] + xn_ld + bps.wr()) if dc == 0 else []
                    last = op("pe", lambda e, dc=dc, pslot=pslot, bank=bank: e.matmul(
                        bank[:, 0:128],
                        ringA[:, (pslot + dc) * 128:(pslot + dc + 1) * 128],
                        xnv(dc, 384, 128), start=(dc == 0), stop=(dc == 15)),
                        w, f"qpro_{oc}_{dc}")
                for s_ in slots:
                    s_.did_read(last)
                bps.did_write(last)
                ev = op("act", lambda e, oc=oc, bank=bank: e.copy(
                    pre_q[:, oc * 515 + 387: oc * 515 + 515], bank[:, 0:128]),
                    [last] + (pbq.wr() if oc == 0 else []), f"qproev_{oc}")
                bps.did_read(ev)
                pre_done.append(ev)
            pbq.did_write(pre_done[-1])
            bsvq = B_(("sv", "q"))
            so_ = op("dve", lambda e: e.tensor_copy(
                sv_q.rearrange("p (oc c) -> p oc c", c=3)[:, :, 0:3],
                pre_q.rearrange("p (oc c) -> p oc c", c=515)[:, :, 512:515]),
                [pre_done[-1]] + bsvq.wr(), "svqH")
            bsvq.did_write(so_)
            pbq.did_read(so_)
            pre_done.append(so_)
        else:
            pre_done = proj_pre("q", "wq", pre_q, copy_pads=True)
            pre_done = pre_done + proj_pre("k", "wk", pre_k, copy_pads=True)

        def conv(which, wname, pre_view, bias_s, dst, bkey):
            evs = []
            lds_c = {}

            def _cld(cg2):
                if cg2 < 16:
                    lds_c[cg2] = statA_load(
                        wbf[wname][cg2].rearrange("p a b -> p (a b)"), 4,
                        [prolog[wname]], f"{wname}l{tagg}_{cg2}")

            _cld(0)
            _cld(1)
            _cld(2)
            for cg in range(16):
                _cld(cg + 3)
                pslot, slots, wl = lds_c.pop(cg)
                bank = psb[2 + (cg % 2)]
                bps = B_(("bank", 2 + cg % 2))
                last = None
                for tau in range(4):
                    w = ([wl, pre_done[-1]] + bps.wr()) if tau == 0 else []
                    last = op("pe", lambda e, cg=cg, tau=tau, pslot=pslot, bank=bank,
                              pre_view=pre_view: e.matmul(
                        bank[:, :],
                        ringA[:, (pslot + tau) * 128:(pslot + tau + 1) * 128],
                        pre_view[:, cg * 515 + tau: cg * 515 + tau + 512],
                        start=(tau == 0), stop=(tau == 3)),
                        w, f"cv{which}{tagg}_{cg}_{tau}")
                for s_ in slots:
                    s_.did_read(last)
                bps.did_write(last)
                db = B_(bkey)
                ev = op("act", lambda e, cg=cg, bank=bank, dst=dst, bias_s=bias_s:
                        e.activation(dst[:, cg * 512: cg * 512 + 512], bank[:, :],
                                     AF.Silu, bias=bias_s[:, cg:cg + 1]),
                        [last] + (db.wr() if cg == 0 else []), f"cve{which}{tagg}_{cg}")
                bps.did_read(ev)
                evs.append(ev)
            B_(bkey).did_write(evs[-1])
            return evs

        if not is_halo:
            q_done = conv("q", "wcq", pre_q, bcq_s, qg, ("qg",))
        else:
            q_done = None
        k_done = conv("k", "wck", pre_k, bck_s, kg, ("kg",))
        if g == 0 and not is_halo:
            t_ = tap("qg0", qg[:, :], [q_done[-1]])
            if t_ is not None:
                B_(("qg",)).did_read(t_)
            t_ = tap("kg0", kg[:, :], [k_done[-1]])
            if t_ is not None:
                B_(("kg",)).did_read(t_)
            t_ = tap("xng0", xn_g[:, :], xn_ld)
            if t_ is not None:
                B_("xng").did_read(t_)
            t_ = tap("preq0", arena[:, :], [pre_done[-1]])

        v_store = []
        vpend = {}

        def vb_ld(vc2, b2):
            if vc2 > 7:
                return
            src_ = wbf["wv"][vc2].rearrange("p a b -> p (a b)")
            vpend[(vc2, b2)] = rhsB_load_multi(
                src_[:, b2 * 2048:(b2 + 1) * 2048], 4, [prolog["wv"]],
                f"wvl{tagg}_{vc2}_{b2}", q="gp", batch=4)

        for b2_ in range(4):
            vb_ld(0, b2_)
        vb_ld(1, 0)
        for vc in range(8):
            banks = [psb[3], psb[4], psb[5], psb[6]]
            bbank = [B_(("bank", 3 + i)) for i in range(4)]
            firstw = []
            for bb_ in bbank:
                firstw += bb_.wr()
            lasts = [None] * 4
            for dc in range(16):
                if dc in (4, 8, 12) and vc < 7:
                    vb_ld(vc + 1, dc // 4)
                sp_, slot, wl = vpend[(vc, dc // 4)][dc % 4]
                for tc in range(4):
                    w = [wl] + ((xn_ld + firstw) if dc == 0 else [])
                    m = op("pe", lambda e, dc=dc, tc=tc, sp_=sp_, banks=banks: e.matmul(
                        banks[tc][:, :], xnv(dc, tc * 128, 128),
                        ringB[:, sp_ * 512:(sp_ + 1) * 512],
                        start=(dc == 0), stop=(dc == 15)), w,
                        f"vm{tagg}_{vc}_{dc}_{tc}")
                    slot.did_read(m)
                    lasts[tc] = m
            vb_ld(vc + 2, 0)
            bvs = B_("vstage")
            evs = []
            for tc in range(4):
                bbank[tc].did_write(lasts[tc])
                ev = op("dve", lambda e, tc=tc, banks=banks: e.tensor_copy(
                    vstage[:, tc * 512:(tc + 1) * 512], banks[tc][:, :]),
                    [lasts[tc]] + (bvs.wr() if tc == 0 else []), f"vev{tagg}_{vc}_{tc}")
                bbank[tc].did_read(ev)
                evs.append(ev)
            bvs.did_write(evs[-1])
            so = dma("sp", vb[gstart:gstart + 512, vc * 512:(vc + 1) * 512]
                     .rearrange("(c p) f -> p c f", p=128),
                     vstage.rearrange("p (c f) -> p c f", c=4), [evs[-1]],
                     f"vst{tagg}_{vc}")
            bvs.did_read(so)
            v_store.append(so)

        bz3, bz4 = B_(("bank", 3)), B_(("bank", 4))
        mg = mb_ = None
        for dc in range(16):
            w = (init_ops + xn_ld + bz3.wr() + v_store) if dc == 0 else []
            mg = op("pe", lambda e, dc=dc: e.matmul(
                psb[3][0:16, :], wg_s[:, dc * 16:(dc + 1) * 16], xnv(dc, 0, 512),
                start=(dc == 0), stop=(dc == 15)), w, f"zg{tagg}_{dc}")
        for dc in range(16):
            w = (init_ops + xn_ld + bz4.wr()) if dc == 0 else []
            mb_ = op("pe", lambda e, dc=dc: e.matmul(
                psb[4][0:16, :], wb_s[:, dc * 16:(dc + 1) * 16], xnv(dc, 0, 512),
                start=(dc == 0), stop=(dc == 15)), w, f"zbm{tagg}_{dc}")
        bz3.did_write(mg)
        bz4.did_write(mb_)
        bzf = B_("zf")
        sgo = op("act", lambda e: e.activation(zg_f[:, :], psb[3][0:16, :], AF.Sigmoid,
                                               bias=bgc_s[:, 0:1]),
                 [mg] + bzf.wr(), f"sg{tagg}")
        sbo = op("act", lambda e: e.activation(zb_f[:, :], psb[4][0:16, :], AF.Sigmoid,
                                               bias=bbc_s[:, 0:1]), [mb_], f"sb{tagg}")
        bz3.did_read(sgo)
        bz4.did_read(sbo)
        lgo = op("act", lambda e: e.activation(lng_f[:, :], zg_f[:, :], AF.Ln),
                 [sgo], f"lg{tagg}")
        lbo = op("act", lambda e: e.activation(lnb_f[:, :], zb_f[:, :], AF.Ln),
                 [sbo], f"lbx{tagg}")
        bzf.did_write(lbo)
        z_done = [lgo, lbo]
        sch.set_join(p_start)

        for ci in range(4):
            cp = ci % 2
            tok0 = gstart + ci * 128
            bvc = B_(("vbuf", cp))
            vload = dma("sp", vbuf[:, cp * 4096:(cp + 1) * 4096],
                        vb[tok0:tok0 + 128, :], v_store + bvc.wr(), f"vld{tagg}_{ci}")
            bvc.did_write(vload)

            bl = B_("ladb")
            blb = B_("lbf")
            c0 = op("dve", lambda e, ci=ci: e.tensor_copy(
                lad[:, 128:256], lnb_f[:, ci * 128:(ci + 1) * 128]),
                z_done + bl.wr() + [zpad2], f"lc{tagg}_{ci}")
            cur = c0
            plan = [(128, 384, 1), (384, 128, 2), (128, 384, 4), (384, 128, 8),
                    (128, 384, 16), (384, 128, 32)]
            for st_i, (src0, dst0, sft) in enumerate(plan):
                cur = op("dve", lambda e, src0=src0, dst0=dst0, sft=sft: e.tensor_add(
                    lad[:, dst0:dst0 + 128], lad[:, src0:src0 + 128],
                    lad[:, src0 - sft:src0 + 128 - sft]), [cur],
                    f"ldd{tagg}_{ci}_{st_i}")
            lbchunk = op("dve", lambda e, ci=ci: e.tensor_add(
                lb_f[:, ci * 128:(ci + 1) * 128], lad[:, 128:256],
                lad[:, 64:192]), [cur] + blb.wr(), f"lbc{tagg}_{ci}")
            bl.did_write(lbchunk)
            blb.did_write(lbchunk)

            bdf = B_(("decf", cp))
            bdt0 = B_("dtmp")
            pd1 = op("dve", lambda e: e.tensor_copy(cstc[:, 3:4], cstc[:, 0:1]),
                     [], f"pad1{tagg}_{ci}")
            pd2 = op("dve", lambda e: e.tensor_copy(cstc[:, 3:4], cstc[:, 1:2]),
                     [pd1], f"pad2{tagg}_{ci}")
            d0 = op("dve", lambda e, ci=ci: e.tensor_scalar(
                dtmp[:, :], lb_f[:, ci * 128:(ci + 1) * 128],
                lb_f[:, ci * 128 + 127:ci * 128 + 128], None,
                ALU.subtract),
                bdt0.wr(), f"dc0{tagg}_{ci}", flush=[lbchunk], drain=True)
            d0 = op("dve", lambda e: e.tensor_scalar(
                dtmp[:, :], dtmp[:, :], -1.0, None, ALU.mult), [d0],
                f"dc0n{tagg}_{ci}")
            bdt0.did_write(d0)
            d1 = op("act", lambda e, cp=cp: e.activation(
                dec_f[:, cp * 128:(cp + 1) * 128], dtmp[:, :],
                AF.Exp), [d0] + bdf.wr(), f"dc1{tagg}_{ci}")
            bdf.did_write(d1)
            bdt0.did_read(d1)
            bac = B_(("aCc", cp))
            a0 = op("act", lambda e, ci=ci, cp=cp: e.activation(
                aC_c[:, cp:cp + 1], lb_f[:, ci * 128 + 127:ci * 128 + 128], AF.Exp),
                [lbchunk] + bac.wr(), f"ac{tagg}_{ci}")
            bac.did_write(a0)

            bt5 = B_(("bank", 5))
            blt = B_(("lbtm", cp))
            t1 = op("pe", lambda e, ci=ci: e.transpose(
                psb[5][:, 0:16], lb_f[:, ci * 128:(ci + 1) * 128],
                identf[0:16, 0:16]),
                [lbchunk] + bt5.wr() + init_ops, f"tlb{tagg}_{ci}")
            e1 = op("dve", lambda e, cp=cp: e.tensor_copy(
                lb_tm[:, cp * 16:(cp + 1) * 16], psb[5][:, 0:16]),
                [t1] + blt.wr(), f"tlbe{tagg}_{ci}")
            blt.did_write(e1)
            t2 = op("pe", lambda e, cp=cp: e.transpose(
                psb[5][:, 16:32], dec_f[:, cp * 128:(cp + 1) * 128],
                identf[0:16, 0:16]),
                [d1, e1], f"tdc{tagg}_{ci}")
            bt5.did_write(t2)
            bdt = B_(("dectm", cp))
            e2 = op("dve", lambda e, cp=cp: e.tensor_copy(
                dec_tm[:, cp * 16:(cp + 1) * 16], psb[5][:, 16:32]),
                [t2] + bdt.wr(), f"tdce{tagg}_{ci}")
            bdt.did_write(e2)
            bt5.did_read(e2)

            bps6 = B_(("bank", 6))
            alast = None
            for h in range(16):
                w = ([a0] + bps6.wr()) if h == 0 else []
                alast = op("pe", lambda e, h=h, cp=cp: e.matmul(
                    psb[6][:, h:h + 1], selb[:, h * 128:(h + 1) * 128],
                    aC_c[:, cp:cp + 1], start=True, stop=True), w,
                    f"acb{tagg}_{ci}_{h}")
            bps6.did_write(alast)
            bab = B_(("acb", cp))
            e3 = op("dve", lambda e, cp=cp: e.tensor_copy(
                acb[:, cp * 16:(cp + 1) * 16], psb[6][:, 0:16]),
                [alast] + bab.wr(), f"acbe{tagg}_{ci}")
            bab.did_write(e3)
            bps6.did_read(e3)

            if not is_halo:
                br1 = B_(("r1", cp))
                r0 = op("dve", lambda e, ci=ci: e.tensor_add(
                    r1_f[:, ci * 128:(ci + 1) * 128], lb_f[:, ci * 128:(ci + 1) * 128],
                    lng_f[:, ci * 128:(ci + 1) * 128]),
                    [lbchunk, lgo] + br1.wr(), f"r1{tagg}_{ci}")
                rh = op("dve", lambda e, ci=ci: e.tensor_copy(
                    r1h[:, ci * 128:(ci + 1) * 128], r1_f[:, ci * 128:(ci + 1) * 128]),
                    [r0], f"r1h{tagg}_{ci}")
                rl = op("dve", lambda e, ci=ci: e.scalar_tensor_tensor(
                    r1l[:, ci * 128:(ci + 1) * 128], r1h[:, ci * 128:(ci + 1) * 128],
                    -1.0, r1_f[:, ci * 128:(ci + 1) * 128], ALU.mult, ALU.add),
                    [rh], f"r1l{tagg}_{ci}")
                br1.did_write(rl)

            for h in range(16):
                hp = h % 2
                ks0 = h * 512 + ci * 128
                kslice = kg[:, ks0:ks0 + 128]
                if not is_halo:
                    qslice = qg[:, ks0:ks0 + 128]
                    bE = B_(("bank", 0))
                    eh1 = op("pe", lambda e, h=h, ci=ci: e.matmul(
                        psb[0][:, 0:128], selb[:, h * 128:(h + 1) * 128],
                        r1h[0:16, ci * 128:(ci + 1) * 128], start=True, stop=False),
                        [rl] + bE.wr(), f"E1{tagg}_{ci}_{h}")
                    eh2 = op("pe", lambda e, h=h, ci=ci: e.matmul(
                        psb[0][:, 0:128], selb[:, h * 128:(h + 1) * 128],
                        r1l[0:16, ci * 128:(ci + 1) * 128], start=False, stop=True),
                        [eh1], f"E2{tagg}_{ci}_{h}")
                    bE.did_write(eh2)
                    bA = B_(("bank", 1))
                    am = op("pe", lambda e, kslice=kslice, qslice=qslice: e.matmul(
                        psb[1][:, 0:128], kslice, qslice, start=True, stop=True),
                        [k_done[-1], q_done[-1]] + bA.wr(), f"Ar{tagg}_{ci}_{h}")
                    bA.did_write(am)
                    btD = B_(("tmpD", hp))
                    td = op("dve", lambda e, hp=hp, cp=cp, h=h: e.scalar_tensor_tensor(
                        tmpD[:, hp * 128:(hp + 1) * 128], psb[0][:, 0:128],
                        lb_tm[:, cp * 16 + h:cp * 16 + h + 1], cmask[:, :],
                        ALU.subtract, ALU.add), [eh2] + btD.wr() + init_ops,
                        f"tD{tagg}_{ci}_{h}", flush=[e1], drain=(h == 0))
                    btD.did_write(td)
                    bE.did_read(td)
                    blt.did_read(td)
                    bDp = B_(("Dp", hp))
                    dp = op("act", lambda e, hp=hp: e.activation(
                        Dp_s[:, hp * 128:(hp + 1) * 128],
                        tmpD[:, hp * 128:(hp + 1) * 128], AF.Exp),
                        [td] + bDp.wr(), f"Dp{tagg}_{ci}_{h}")
                    bDp.did_write(dp)
                    btD.did_read(dp)
                    bEq = B_(("Eq", hp))
                    eq = op("act", lambda e, hp=hp: e.activation(
                        Eq_s[:, hp * 128:(hp + 1) * 128], psb[0][:, 0:128], AF.Exp),
                        [eh2] + bEq.wr(), f"Eq{tagg}_{ci}_{h}")
                    bEq.did_write(eq)
                    bE.did_read(eq)
                    bAb = B_(("Abf", hp))
                    ab = op("dve", lambda e, hp=hp: e.tensor_mul(
                        A_bf[:, hp * 128:(hp + 1) * 128], psb[1][:, 0:128],
                        Dp_s[:, hp * 128:(hp + 1) * 128]),
                        [am, dp] + bAb.wr(), f"Ab{tagg}_{ci}_{h}")
                    bAb.did_write(ab)
                    bA.did_read(ab)
                    bDp.did_read(ab)
                    bQb = B_(("Qbf", hp))
                    qb = op("dve", lambda e, hp=hp, qslice=qslice: e.tensor_mul(
                        Qb_bf[:, hp * 128:(hp + 1) * 128], qslice,
                        Eq_s[:, hp * 128:(hp + 1) * 128]),
                        [eq] + bQb.wr(), f"Qb{tagg}_{ci}_{h}")
                    bQb.did_write(qb)
                    bEq.did_read(qb)
                    bSf = B_("Sfull")
                    bSb = B_("Sbfb")
                    scast = op("dve", lambda e, h=h: e.tensor_copy(
                        S_bf[:, :], S_fp[:, h * 256:(h + 1) * 256]),
                        bSb.wr(), f"Sc{tagg}_{ci}_{h}", flush=bSf.rd())
                    bSb.did_write(scast)
                    bSf.did_read(scast)
                    bO = B_(("bank", 2))
                    olast = None
                    ow = bO.wr()
                    for eh_ in range(2):
                        m1_ = op("pe", lambda e, eh_=eh_, hp=hp, h=h, cp=cp: e.matmul(
                            psb[2][:, eh_ * 128:(eh_ + 1) * 128],
                            vbuf[:, cp * 4096 + h * 256 + eh_ * 128:
                                 cp * 4096 + h * 256 + eh_ * 128 + 128],
                            A_bf[:, hp * 128:(hp + 1) * 128], start=True, stop=False),
                            [ab, vload] + ow, f"o1{tagg}_{ci}_{h}_{eh_}")
                        olast = op("pe", lambda e, eh_=eh_, hp=hp: e.matmul(
                            psb[2][:, eh_ * 128:(eh_ + 1) * 128],
                            S_bf[:, eh_ * 128:(eh_ + 1) * 128],
                            Qb_bf[:, hp * 128:(hp + 1) * 128], start=False, stop=True),
                            [qb, scast, m1_], f"o2{tagg}_{ci}_{h}_{eh_}")
                        ow = []
                    bO.did_write(olast)
                    bvc.did_read(olast)
                    bSb.did_read(olast)
                    bAb.did_read(olast)
                    bQb.did_read(olast)
                    ev1 = op("act", lambda e, h=h, ci=ci: e.copy(
                        o_grp[:, (2 * h) * 512 + ci * 128:
                              (2 * h) * 512 + ci * 128 + 128],
                        psb[2][:, 0:128]), [olast], f"oe1{tagg}_{ci}_{h}")
                    ev2 = op("act", lambda e, h=h, ci=ci: e.copy(
                        o_grp[:, (2 * h + 1) * 512 + ci * 128:
                              (2 * h + 1) * 512 + ci * 128 + 128],
                        psb[2][:, 128:256]), [olast], f"oe2{tagg}_{ci}_{h}")
                    bO.did_read(ev1)
                    bO.did_read(ev2)
                    B_("ogrp").did_read(ev2)

                bpt = B_(("bank", 7))
                kt = op("pe", lambda e, kslice=kslice: e.transpose(
                    psT[:, 0:128], kslice, ident[:, :]),
                    [k_done[-1]] + bpt.wr() + init_ops, f"Kt{tagg}_{ci}_{h}")
                bpt.did_write(kt)
                bKh = B_(("Khb", hp))
                kh = op("dve", lambda e, hp=hp, cp=cp, h=h: e.tensor_scalar(
                    Kh_bf[:, hp * 128:(hp + 1) * 128], psT[:, 0:128],
                    dec_tm[:, cp * 16 + h:cp * 16 + h + 1], None, ALU.mult),
                    [kt] + bKh.wr(), f"Kh{tagg}_{ci}_{h}", flush=[e2],
                    drain=(h == 0))
                bKh.did_write(kh)
                bpt.did_read(kh)
                bdt.did_read(kh)
                bSd = B_(("bank", 3))
                sd = op("pe", lambda e, hp=hp, cp=cp, h=h: e.matmul(
                    psb[3][:, 0:256], Kh_bf[:, hp * 128:(hp + 1) * 128],
                    vbuf[:, cp * 4096 + h * 256: cp * 4096 + (h + 1) * 256],
                    start=True, stop=True), [kh, vload] + bSd.wr(),
                    f"Sd{tagg}_{ci}_{h}")
                bSd.did_write(sd)
                bvc.did_read(sd)
                bKh.did_read(sd)
                bSf = B_("Sfull")
                su = op("dve", lambda e, h=h, cp=cp: e.scalar_tensor_tensor(
                    S_fp[:, h * 256:(h + 1) * 256], S_fp[:, h * 256:(h + 1) * 256],
                    acb[:, cp * 16 + h:cp * 16 + h + 1], psb[3][:, 0:256],
                    ALU.mult, ALU.add), [sd] + bSf.wr(), f"Su{tagg}_{ci}_{h}",
                    flush=[e3], drain=(h == 0))
                bSf.did_write(su)
                bSd.did_read(su)
                bab.did_read(su)

        if is_halo:
            sch.set_join(p_start)
            bSf = B_("Sfull")
            mo = op("dve", lambda e: e.tensor_scalar(
                S_fp[:, :], S_fp[:, :], hmk[:, 0:1], None, ALU.mult),
                bSf.wr() + init_ops, "hmaskS")
            bSf.did_write(mo)
            for tn_, tv_ in [("lbH", lb_f[:, :]), ("lnbH", lnb_f[:, :]),
                             ("decH", dec_f[:, :]), ("dectmH", dec_tm[:, :]),
                             ("acbH", acb[:, :]), ("KhH", Kh_bf[:, :]),
                             ("vbufH", vbuf[:, :]), ("zbH", zb_f[:, :])]:
                tap(tn_, tv_, [mo])
            t_ = tap("SH", S_fp[:, :], [mo])
            if t_ is not None:
                bSf.did_read(t_)
            return

        sch.set_join(p_start)
        if g == 0:
            t_ = tap("ogrp0", o_grp[:, :], [])
            if t_ is not None:
                B_("ogrp").did_read(t_)
        if g == 3:
            t_ = tap("Sfin", S_fp[:, :], [])
        wopend = {}

        def wo_ld(cc2, oc2):
            if cc2 > 3 or oc2 >= 32:
                return
            wo2_ = wbf["wo"][cc2].rearrange("p a b -> p (a b)")
            wopend[(cc2, oc2)] = rhsB_load_multi(
                wo2_[:, oc2 * 512:(oc2 + 4) * 512], 4, [P("wo")],
                f"wol{tagg}_{cc2}_{oc2}", q="gp", batch=4)

        wo_ld(0, 0)
        wo_ld(0, 4)
        wo_ld(0, 8)
        for cc in range(4):
            banks = [psb[3], psb[4], psb[5], psb[6]]
            btr = [B_(("bank", 3 + i)) for i in range(4)]
            lasts = [None] * 4
            for oc in range(32):
                if oc % 4 == 0:
                    nc_, no_ = cc, oc + 12
                    if no_ >= 32:
                        nc_, no_ = cc + 1, no_ - 32
                    wo_ld(nc_, no_)
                sp_, slot, wl = wopend[(cc, (oc // 4) * 4)][oc % 4]
                for tc in range(4):
                    w = [wl] + (btr[tc].wr() if oc == 0 else [])
                    m = op("pe", lambda e, oc=oc, tc=tc, sp_=sp_, banks=banks: e.matmul(
                        banks[tc][:, :],
                        o_grp[:, oc * 512 + tc * 128: oc * 512 + tc * 128 + 128],
                        ringB[:, sp_ * 512:(sp_ + 1) * 512],
                        start=(oc == 0), stop=(oc == 31)), w,
                        f"wom{tagg}_{cc}_{oc}_{tc}")
                    slot.did_read(m)
                    lasts[tc] = m
            for tc in range(4):
                btr[tc].did_write(lasts[tc])
                pv = qg_f32[:, (tc % 2) * 512:(tc % 2) * 512 + 512]
                xv2 = qg_f32[:, 1024 + (tc % 2) * 512:1024 + (tc % 2) * 512 + 512]
                bpv = B_(("pv", tc % 2))
                bxv = B_(("xv", tc % 2))
                row0 = 512 + g * 512 + tc * 128
                xl = dma("sp", xv2, xc[row0:row0 + 128, cc * 512:(cc + 1) * 512],
                         bxv.wr(), f"xl{tagg}_{cc}_{tc}")
                bxv.did_write(xl)
                ad = op("dve", lambda e, tc=tc, pv=pv, xv2=xv2, banks=banks:
                        e.tensor_add(pv, banks[tc][:, :], xv2),
                        [lasts[tc], xl] + bpv.wr(), f"pa{tagg}_{cc}_{tc}")
                btr[tc].did_read(ad)
                bxv.did_read(ad)
                bpv.did_write(ad)
                st_ = dma("sp", partial[g * 512 + tc * 128: g * 512 + tc * 128 + 128,
                                        cc * 512:(cc + 1) * 512], pv, [ad],
                          f"pst{tagg}_{cc}_{tc}", ded="partial")
                bpv.did_read(st_)
                B_("partial").did_read(st_)
        sch.set_join(p_start)

    emit_group(0, is_halo=True)
    for g in range(4):
        emit_group(g, is_halo=False)

    p1_end = len(sch.ops)
    sch.set_join(0)

    for gd in reversed(guards[p01_guards_start:]):
        gd.__exit__(None, None, None)
    del guards[p01_guards_start:]

    acc = sb("acc", [128, 8 * 2048], F32)
    ringA2 = sb("ringA2", [128, 96 * 128], BF16)
    RA2 = 96
    ringA2B = [B_(("rA2", i)) for i in range(RA2)]
    rpos["A2"] = 0

    def statA2_load(src_ap, nt, waits, tag):
        pump_cast(2)
        p = rpos["A2"]
        if p + nt > RA2:
            p = 0
        rpos["A2"] = p + nt
        slots = ringA2B[p:p + nt]
        w = list(waits)
        for s_ in slots:
            w += s_.wr()
        o = dma("gp", ringA2[:, p * 128:(p + nt) * 128], src_ap, w, tag)
        for s_ in slots:
            s_.did_write(o)
        return p, slots, o
    harena = sb("harena", [128, 11 * 1024], BF16)
    xn_h = sb("xnh", [128, 16 * 1024], BF16)
    sg_s = sb("sgs", [128, 512], F32)
    sc_s = sb("scs", [128, 8 * 8], F32)
    bi_s = sb("bis", [128, 8 * 8], F32)
    b2_s = sb("b2s", [128, 8 * 8], F32)
    mk_s = sb("mks", [128, 8 * 8], F32)
    mk2_s = sb("mk2s", [128, 8 * 8], F32)
    mm_s = sb("mmscol", [128, 8 * 8], F32)
    we_s = sb("wes", [128, 64], F32)
    pbuf = sb("pbuf", [128, 2048], F32)
    obuf = sb("obuf", [128, 2048], F32)
    _pcnt = [0]

    for half in range(2):
        h_start = len(sch.ops)
        htok = 512 + half * 1024

        bxh = B_("xnh")
        xn_ld2 = []
        for dc in range(16):
            o = dma("sp", xn_h[:, dc * 1024:(dc + 1) * 1024],
                    xnb[htok:htok + 1024, dc * 128:(dc + 1) * 128],
                    ln_wait(htok, htok + 1024) + (bxh.wr() if dc == 0 else []),
                    f"xnl2_{half}_{dc}", transpose=True)
            xn_ld2.append(o)
        bxh.did_write(xn_ld2[-1])

        def xnv2(dc, lo, n):
            return xn_h[:, dc * 1024 + lo: dc * 1024 + lo + n]

        sig_done = []
        for tt in range(8):
            brt = B_(("bank", 4))
            last = None
            for dc in range(16):
                w = (xn_ld2 + init_ops + brt.wr()) if dc == 0 else []
                last = op("pe", lambda e, dc=dc, tt=tt: e.matmul(
                    psb[4][:, 0:8], xnv2(dc, tt * 128, 128),
                    wgt_s[:, dc * 8:(dc + 1) * 8], start=(dc == 0), stop=(dc == 15)),
                    w, f"rt{half}_{tt}_{dc}")
            brt.did_write(last)
            q0 = op("act", lambda e, tt=tt: e.activation(
                sc_s[:, tt * 8:(tt + 1) * 8], psb[4][:, 0:8], AF.Sigmoid),
                [last] + B_(("rts", tt)).wr(), f"rts{half}_{tt}")
            brt.did_read(q0)
            B_(("rts", tt)).did_write(q0)
            sig_done.append(q0)

        def rstep(name, fn_make, waits_extra=(), drain=False):
            outs = []
            for tt in range(8):
                o_ = op("dve", fn_make(tt), list(waits_extra), f"{name}{half}_{tt}",
                        drain=(drain and tt == 0))
                outs.append(o_)
            return outs

        s1o = rstep("rb", lambda tt: (lambda e, tt=tt: e.tensor_add(
            bi_s[:, tt * 8:(tt + 1) * 8], sc_s[:, tt * 8:(tt + 1) * 8],
            gb_b[:, :])), sig_done + init_ops)
        s2o = rstep("rm1", lambda tt: (lambda e, tt=tt: e.reduce_max(
            mm_s[:, tt * 8:tt * 8 + 1], bi_s[:, tt * 8:(tt + 1) * 8], axis=AX.X)))
        s3o = rstep("rk1", lambda tt: (lambda e, tt=tt: e.tensor_scalar(
            mk_s[:, tt * 8:(tt + 1) * 8], bi_s[:, tt * 8:(tt + 1) * 8],
            mm_s[:, tt * 8:tt * 8 + 1], None, ALU.is_ge)), drain=True)
        s4o = rstep("rb2", lambda tt: (lambda e, tt=tt: e.scalar_tensor_tensor(
            b2_s[:, tt * 8:(tt + 1) * 8], mk_s[:, tt * 8:(tt + 1) * 8], NEG,
            bi_s[:, tt * 8:(tt + 1) * 8], ALU.mult, ALU.add)))
        s5o = rstep("rm2", lambda tt: (lambda e, tt=tt: e.reduce_max(
            mm_s[:, tt * 8 + 1:tt * 8 + 2], b2_s[:, tt * 8:(tt + 1) * 8],
            axis=AX.X)))
        s6o = rstep("rk2", lambda tt: (lambda e, tt=tt: e.tensor_scalar(
            mk2_s[:, tt * 8:(tt + 1) * 8], b2_s[:, tt * 8:(tt + 1) * 8],
            mm_s[:, tt * 8 + 1:tt * 8 + 2], None, ALU.is_ge)), drain=True)
        s7o = rstep("rks", lambda tt: (lambda e, tt=tt: e.tensor_add(
            mk_s[:, tt * 8:(tt + 1) * 8], mk_s[:, tt * 8:(tt + 1) * 8],
            mk2_s[:, tt * 8:(tt + 1) * 8])))
        s8o = rstep("rw", lambda tt: (lambda e, tt=tt: e.tensor_mul(
            mk2_s[:, tt * 8:(tt + 1) * 8], sc_s[:, tt * 8:(tt + 1) * 8],
            mk_s[:, tt * 8:(tt + 1) * 8])))
        s9o = rstep("rsum", lambda tt: (lambda e, tt=tt: e.reduce_sum(
            mm_s[:, tt * 8 + 2:tt * 8 + 3], mk2_s[:, tt * 8:(tt + 1) * 8],
            axis=AX.X)))
        sao = rstep("rrc", lambda tt: (lambda e, tt=tt: e.reciprocal(
            mm_s[:, tt * 8 + 3:tt * 8 + 4], mm_s[:, tt * 8 + 2:tt * 8 + 3])))
        bwe = B_("wes")
        rt_done = []
        for tt in range(8):
            o_ = op("dve", lambda e, tt=tt: e.tensor_scalar(
                we_s[:, tt * 8:(tt + 1) * 8], mk2_s[:, tt * 8:(tt + 1) * 8],
                mm_s[:, tt * 8 + 3:tt * 8 + 4], None, ALU.mult),
                (bwe.wr() if tt == 0 else []), f"rwe{half}_{tt}",
                flush=[sao[tt]], drain=(tt == 0))
            rt_done.append(o_)
        B_("wes").did_write(rt_done[-1])
        if half == 0:
            tap("rta", bi_s[:, 0:8], [s1o[0]])
            tap("rtb", mk_s[:, 0:8], [s7o[0]])
            tap("rtc", mk2_s[:, 0:8], [s8o[0]])
            tap("rtd", mm_s[:, 0:8], [sao[0]])


        def up_tile_pre(ld, t_lo, t_n, bank, bkey, tag):
            pslot, slots, wl = ld
            bps = B_(bkey)
            last = None
            for dc in range(16):
                w = ([wl] + xn_ld2 + bps.wr()) if dc == 0 else []
                last = op("pe", lambda e, dc=dc, pslot=pslot, bank=bank,
                          t_lo=t_lo, t_n=t_n: e.matmul(
                    bank[:, 0:t_n],
                    ringA2[:, (pslot + dc) * 128:(pslot + dc + 1) * 128],
                    xnv2(dc, t_lo, t_n), start=(dc == 0), stop=(dc == 15)),
                    w, f"{tag}_{dc}")
            for s_ in slots:
                s_.did_read(last)
            bps.did_write(last)
            return last

        def up_pair_pre(ldy, ldg, t_lo, t_n, hcol, tag):
            pi = _pcnt[0] % 2
            _pcnt[0] += 1
            yb, gb = (0, 1) if pi == 0 else (2, 3)
            myl = up_tile_pre(ldy, t_lo, t_n, psb[yb], ("bank", yb), tag + "y")
            mgl = up_tile_pre(ldg, t_lo, t_n, psb[gb], ("bank", gb), tag + "g")
            bsg = B_("sgs")
            sgv = sg_s[:, 0:t_n]
            sl = op("act", lambda e, t_n=t_n, gb=gb, sgv=sgv: e.activation(
                sgv, psb[gb][:, 0:t_n], AF.Silu),
                [mgl] + bsg.wr(), tag + "s")
            bsg.did_write(sl)
            B_(("bank", gb)).did_read(sl)
            bha = B_("harena")
            hv = op("dve", lambda e, hcol=hcol, t_n=t_n, yb=yb, sgv=sgv:
                    e.tensor_mul(
                harena[:, hcol: hcol + t_n], sgv, psb[yb][:, 0:t_n]),
                [myl, sl] + bha.wr(), tag + "h")
            bsg.did_read(hv)
            B_(("bank", yb)).did_read(hv)
            bha.did_write(hv)
            return hv

        def up_pair(apy, apg, wwait, t_lo, t_n, hcol, tag):
            ldy = statA_load(apy, 16, wwait, tag + "y_l")
            ldg = statA_load(apg, 16, wwait, tag + "g_l")
            return up_pair_pre(ldy, ldg, t_lo, t_n, hcol, tag)

        def down_block(cc, tiles, nhc, hcols, tts, mode, tag):
            banks = [psb[3], psb[4], psb[5], psb[6]]
            btr = [B_(("bank", 3 + i)) for i in range(4)]
            lasts = [None] * 4
            bha = B_("harena")
            for hcx in range(nhc):
                sp_, slot, wl = tiles[hcx]
                for tc in range(4):
                    w = [wl] + bha.rd() + (btr[tc].wr() if hcx == 0 else [])
                    m = op("pe", lambda e, sp_=sp_, tc=tc, hcx=hcx, nhc=nhc,
                           banks=banks, hcols=hcols: e.matmul(
                        banks[tc][:, :], harena[:, hcols[hcx] + tc * 128:
                                                hcols[hcx] + tc * 128 + 128],
                        ringB[:, sp_ * 512:(sp_ + 1) * 512],
                        start=(hcx == 0), stop=(hcx == nhc - 1)), w,
                        f"{tag}_{hcx}_{tc}")
                    slot.did_read(m)
                    bha.did_read(m)
                    lasts[tc] = m
            for tc in range(4):
                btr[tc].did_write(lasts[tc])
                tt = tts[tc]
                bac2 = B_(("accb", tt, cc))
                av = acc[:, tt * 2048 + cc * 512: tt * 2048 + cc * 512 + 512]
                if mode == "copy":
                    ev = op("dve", lambda e, av=av, tc=tc, banks=banks:
                            e.tensor_copy(av, banks[tc][:, :]),
                            [lasts[tc]] + bac2.wr(), f"{tag}_ev{tc}")
                elif mode == "add":
                    ev = op("dve", lambda e, av=av, tc=tc, banks=banks:
                            e.tensor_add(av, banks[tc][:, :], av),
                            [lasts[tc]] + bac2.wr(), f"{tag}_ev{tc}")
                else:
                    ev = op("dve", lambda e, av=av, tc=tc, tt=tt, ex=mode,
                            banks=banks: e.scalar_tensor_tensor(
                        av, banks[tc][:, :], we_s[:, tt * 8 + ex: tt * 8 + ex + 1],
                        av, ALU.mult, ALU.add),
                        [lasts[tc], rt_done[tt]] + bac2.wr(), f"{tag}_ev{tc}",
                        drain=(tc == 0))
                btr[tc].did_read(ev)
                bac2.did_write(ev)

        for tq in range(2):
            t_lo = tq * 512
            lds_s = {}

            def shu_ld(p_, tq=tq, lds_s=lds_s):
                lds_s[p_] = (
                    statA2_load(wbf["sgw"][p_].rearrange("p a b -> p (a b)"),
                                16, [P("sgw")], f"shuly{half}_{tq}_{p_}"),
                    statA2_load(wbf["sgw"][22 + p_].rearrange(
                        "p a b -> p (a b)"), 16, [P("sgw")],
                        f"shulg{half}_{tq}_{p_}"))

            shu_ld(0)
            shu_ld(1)
            for p_ in range(22):
                if p_ + 2 < 22:
                    shu_ld(p_ + 2)
                ldy, ldg = lds_s.pop(p_)
                up_pair_pre(ldy, ldg, t_lo, 512, p_ * 512,
                            f"shu{half}_{tq}_{p_}")
            for cc in range(4):
                tiles = rhsB_load_multi(wbf["sdw"][cc].rearrange(
                    "p a b -> p (a b)"), 11, [P("sdw")],
                    f"shdl{half}_{tq}_{cc}", q="gp", batch=4)
                down_block(cc, tiles, 11, [hx * 512 for hx in range(11)],
                           [tq * 4 + tc for tc in range(4)], "copy",
                           f"shd{half}_{tq}_{cc}a")
                tiles2 = rhsB_load_multi(wbf["sdw"][cc].rearrange(
                    "p a b -> p (a b)")[:, 11 * 512:], 11, [P("sdw")],
                    f"shdl2{half}_{tq}_{cc}", q="gp", batch=4)
                down_block(cc, tiles2, 11,
                           [(11 + hx) * 512 for hx in range(11)],
                           [tq * 4 + tc for tc in range(4)], "add",
                           f"shd{half}_{tq}_{cc}b")

        if half == 0 and "accsh" in DBG:
            acw_ = [B_(("accb", t2_, c2_)).w for t2_ in range(8)
                    for c2_ in range(4)]
            t_ = tap("accsh", acc[:, :], acw_)
            for t2_ in range(8):
                for c2_ in range(4):
                    B_(("accb", t2_, c2_)).did_read(t_)

        ex_src = [(ex, p_) for ex in range(8) for p_ in range(11)]
        ex_lds = {}

        def ex_ld(i):
            ex, p_ = ex_src[i]
            ex_lds[i] = (
                statA2_load(wbf["wup"][ex, 11 + p_].rearrange(
                    "p a b -> p (a b)"), 16, [P("wup")],
                    f"exuly{half}_{ex}_{p_}"),
                statA2_load(wbf["wup"][ex, p_].rearrange(
                    "p a b -> p (a b)"), 16, [P("wup")],
                    f"exulg{half}_{ex}_{p_}"))

        ex_ld(0)
        ex_ld(1)
        idx = 0
        for ex in range(8):
            for p_ in range(11):
                if idx + 2 < len(ex_src):
                    ex_ld(idx + 2)
                ldy, ldg = ex_lds.pop(idx)
                for t5 in range(2):
                    up_pair_pre(ldy, ldg, t5 * 512, 512,
                                p_ * 1024 + t5 * 512,
                                f"exu{half}_{ex}_{t5}_{p_}")
                idx += 1
            for cc in range(4):
                tiles = rhsB_load_multi(wbf["wdn"][ex, cc].rearrange(
                    "p a b -> p (a b)"), 11, [P("wdn")],
                    f"exdl{half}_{ex}_{cc}", q="gp", batch=4)
                for sub in range(2):
                    down_block(cc, tiles, 11,
                               [hx * 1024 + sub * 512 for hx in range(11)],
                               [sub * 4 + tc for tc in range(4)], ex,
                               f"exd{half}_{ex}_{cc}_{sub}")

        if half == 0:
            t_ = tap("wes0", we_s[:, :], [rt_done[-1]])
            acw = [B_(("accb", t2_, c2_)).w for t2_ in range(8) for c2_ in range(4)]
            t_ = tap("acc0", acc[:, :], acw)
            if t_ is not None:
                for t2_ in range(8):
                    for c2_ in range(4):
                        B_(("accb", t2_, c2_)).did_read(t_)

        for tt in range(8):
            bpb = B_("pbuf")
            bob = B_("obuf")
            pl = dma("sp", pbuf[:, :],
                     partial[half * 1024 + tt * 128: half * 1024 + tt * 128 + 128, :],
                     bpb.wr() + B_("partial").rd(), f"pl{half}_{tt}")
            bpb.did_write(pl)
            ac_waits = [B_(("accb", tt, cc)).w for cc in range(4)]
            ao = op("dve", lambda e, tt=tt: e.tensor_add(
                obuf[:, :], acc[:, tt * 2048:(tt + 1) * 2048], pbuf[:, :]),
                [pl] + ac_waits + bob.wr(), f"fin{half}_{tt}")
            bpb.did_read(ao)
            for cc in range(4):
                B_(("accb", tt, cc)).did_read(ao)
            bob.did_write(ao)
            so = dma("sp", out_d[half * 1024 + tt * 128: half * 1024 + tt * 128 + 128, :],
                     obuf[:, :], [ao], f"os{half}_{tt}", ded="outs")
            bob.did_read(so)
            B_("outd").did_read(so)
        sch.set_join(h_start)

    ftaps = [tap("xnb", xnb[:, :], ln_done), tap("vb", vb[:, :], []),
             tap("partial", partial[:, :], [])]
    if "xend" in DBG:
        dma("sp", dbg_outs["xend"], xc[0:128, :], [sch.ops[-1]], "tap_xend")

    fins = sch.final_ops()
    sch.add("pe", None, fins, "final_wait_pe")
    sch.add("dve", None, fins, "final_wait_dve")

    import contextlib
    with contextlib.ExitStack() as est:
        sems = {}
        for nm in sorted(sch.sem_names):
            sems[nm] = est.enter_context(nc.semaphore("s_" + nm))
        import kernel as _K
        _K.SEM_NUMS = {nm: s.num for nm, s in sems.items()}
        s_gate = est.enter_context(nc.semaphore("s_gate"))
        nums = sorted(s_.num for s_ in sems.values())
        srange = range(nums[0], nums[-1] + 1)
        M_ = 1 << 20
        nc.gpsimd.sem_clear(srange)
        nc.gpsimd.dma_reset(srange)
        nc.gpsimd.sem_clear(srange)
        nc.gpsimd.sem_inc(s_gate, M_)
        for eng_ in (nc.tensor, nc.vector, nc.scalar, nc.sync):
            eng_.wait_ge(s_gate, M_)
            eng_.sem_inc(s_gate, M_)
            eng_.wait_ge(s_gate, 6 * M_)
        nc.gpsimd.wait_ge(s_gate, 5 * M_)
        nc.gpsimd.sem_clear(srange)
        nc.gpsimd.sem_inc(s_gate, M_)
        block = nc.Block()
        block.__enter__()

        @block.tensor
        def _(e):
            sch.emit("pe", e, sems)

        @block.vector
        def _(e):
            sch.emit("dve", e, sems)

        @block.scalar
        def _(e):
            sch.emit(("act", "gp"), e, sems)

        @block.sync
        def _(e):
            sch.emit("sp", e, sems)

        @block.gpsimd
        def _(e):
            sch.emit("ac", e, sems)

        block.__exit__(None, None, None)
        nc.all_engine_barrier()
        nc.gpsimd.sem_clear(srange)
        nc.all_engine_barrier()

    for gd in reversed(guards):
        gd.__exit__(None, None, None)
    import kernel as _K
    _K.PE_TAGS = [o.tag for o in sch.ops if o.unit == "pe" and o.fn is not None]
    return nc


def kernel(**inputs):
    st = _stage_weights(inputs)
    x = np.asarray(inputs["L0_attn_hidden_states"], dtype=np.float32)
    nc = build_program()
    in_maps = []
    for c in range(8):
        b, q = divmod(c, 4)
        t0 = q * TQ
        lo = t0 - HALO
        if lo < 0:
            xcore = np.concatenate(
                [np.zeros((-lo, D), np.float32), x[b, 0:t0 + TQ]], axis=0)
        else:
            xcore = np.ascontiguousarray(x[b, lo:t0 + TQ])
        hm = np.full((128, 1), 0.0 if q == 0 else 1.0, np.float32)
        m = {"xc": xcore, "hmask": hm, "ident": st["ident"], "identf": st["identf"],
             "ones": st["ones"], "sel": st["sel"]}
        for nm, _ in _WSHAPES:
            m[nm] = st[nm]
        in_maps.append(m)
    res = run_bass_kernel_spmd(nc, in_maps, core_ids=list(range(8)))
    out = np.empty((B, S, D), np.float32)
    for c in range(8):
        b, q = divmod(c, 4)
        out[b, q * TQ:(q + 1) * TQ] = res.results[c]["out"]
    return out
